# version 07
# baseline (speedup 1.0000x reference)
import contextlib

import numpy as np
import ml_dtypes

import concourse.bass as bass
import concourse.mybir as mybir
from concourse.bass_utils import run_bass_kernel_spmd

F32 = mybir.dt.float32
BF16 = mybir.dt.bfloat16
AF = mybir.ActivationFunctionType
ALU = mybir.AluOpType
AXX = mybir.AxisListType.X

NCORES = 8
D = 2048
NF = 16
TOK = 1024
POS = 512
NTT = 8
H = 16
NKT = 32
NHG = 4
S = 4096
LN_EPS = 1e-5
ATTN_SCALE = 0.125
FFN_SCALE = 0.125
LAMBDA_INIT = 0.8

_PHASES = ("ffn1", "attn", "ffn2")
_DEBUG = ()
_FFN_LIMIT = None
_FFN_SKIP_ROUTER = False
_FFN_SKIP_VEVAC = False
_DUMPS = ()
_ATTN_SERIAL = False
_Q_SERIAL = False

_OWNER = {"mm": "tensor", "smm": "tensor", "zav": "tensor",
          "act": "scalar", "sxp": "scalar",
          "dve": "vector",
          "wld": "gpsimd", "cc": "gpsimd",
          "sld": "sync"}


class Rec:
    ENGINES = ("sync", "gpsimd", "tensor", "scalar", "vector")

    def __init__(self, nc, stack):
        self.nc = nc
        self.stack = stack
        self.ops = {e: [] for e in self.ENGINES}
        self.sems = {}
        self.cnt = {}
        self.issued = {e: {} for e in self.ENGINES}

    def sem(self, name):
        if name not in self.sems:
            self.sems[name] = self.stack.enter_context(self.nc.semaphore(name))
            self.cnt.setdefault(name, 0)
        return self.sems[name]

    def count(self, name):
        return self.cnt.get(name, 0)

    def emit(self, engine, fn, waits=None, inc=None):
        if waits is None:
            waits = [(ch, self.cnt[ch]) for ch in self.cnt
                     if _OWNER[ch] != engine and self.cnt[ch] > 0]
        else:
            norm = []
            for wspec in waits:
                if isinstance(wspec, tuple):
                    norm.append(wspec)
                else:
                    norm.append((wspec, self.cnt.get(wspec, 0)))
            waits = [(n, v) for (n, v) in norm if v > 0]
        if engine == "vector":
            waits = list(waits) + [("dve", self.cnt.get("dve", 0))]
        wl = []
        iss = self.issued[engine]
        for nm, val in waits:
            if iss.get(nm, 0) >= val:
                continue
            iss[nm] = val
            wl.append((self.sem(nm), val))
        if inc is not None:
            self.sem(inc[0])
            self.cnt[inc[0]] += inc[1]
            inc = (self.sems[inc[0]], inc[1])

        def op(eng, fn=fn, wl=wl, inc=inc):
            for s, v in wl:
                eng.wait_ge(s, v)
            ins = fn(eng)
            if inc is not None:
                assert ins is not None
                ins.then_inc(inc[0], inc[1])
        self.ops[engine].append(op)

    def replay(self, block):
        for e in self.ENGINES:
            ops = self.ops[e]
            self.ops[e] = []
            if not ops:
                continue

            def body(eng, ops=ops):
                for op in ops:
                    op(eng)
            getattr(block, e)(body)


class WStream:
    def __init__(self, rec, wbufs):
        self.rec, self.wbufs = rec, wbufs
        self.mm_after_use = [rec.count("mm")] * len(wbufs)
        self.j = 0

    def load(self, fn):
        s = self.j % len(self.wbufs)
        self.j += 1
        wb = self.wbufs[s]
        self.rec.emit("gpsimd", lambda e, wb=wb, fn=fn: fn(e, wb),
                      waits=[("mm", self.mm_after_use[s])], inc=("wld", 16))
        return s, wb, self.rec.count("wld")

    def used(self, s):
        self.mm_after_use[s] = self.rec.count("mm")


class ProjPipe:
    def __init__(self, rec, banks):
        self.rec, self.banks = rec, banks
        self.act_e = [rec.count("act")] * len(banks)
        self.dve_e = [rec.count("dve")] * len(banks)
        self.i = 0

    def floor(self):
        a, d = self.rec.count("act"), self.rec.count("dve")
        self.act_e = [max(x, a) for x in self.act_e]
        self.dve_e = [max(x, d) for x in self.dve_e]

    def group(self, mk_mm, nchunks, wld=0, extra=()):
        b = self.i % len(self.banks)
        self.i += 1
        pp = self.banks[b]
        for f in range(nchunks):
            w = ([("wld", wld), ("act", self.act_e[b]), ("dve", self.dve_e[b])]
                 + list(extra)) if f == 0 else []
            self.rec.emit("tensor", mk_mm(f, pp), waits=w,
                          inc=(("mm", 1) if f == nchunks - 1 else None))
        return b, pp

    def done(self, b):
        self.act_e[b] = self.rec.count("act")
        self.dve_e[b] = self.rec.count("dve")


def emit_ln(rec, g, out_tiles=None):
    resid, onesf32, ones1x128 = g["resid"], g["onesf32"], g["ones1x128"]
    sq = g["lnsq"]
    prs, prq, pbc = g["prs"], g["prq"], g["pbc"]
    mrow, vrow, rrow, srow = g["lnrows"]
    meanB, rstdB = g["meanB"], g["rstdB"]
    epsrow = g["epsrow"]

    for half in range(2):
        c0 = half * 512
        mmq0 = rec.count("mm")
        for f in range(NF):
            rec.emit("scalar", lambda e, f=f, c0=c0: e.activation(
                sq[f % 2][:, c0:c0 + 512], resid[f][:, c0:c0 + 512], AF.Square),
                waits=(None if f < 2 else [("mm", mmq0 + f - 1)]),
                inc=("act", 1))
            rec.emit("tensor", lambda e, f=f, c0=c0: e.matmul(
                prs[:], onesf32[:], resid[f][:, c0:c0 + 512],
                start=(f == 0), stop=(f == NF - 1)),
                waits=(None if f == 0 else []),
                inc=None)
            rec.emit("tensor", lambda e, f=f, c0=c0: e.matmul(
                prq[:], onesf32[:], sq[f % 2][:, c0:c0 + 512],
                start=(f == 0), stop=(f == NF - 1)),
                waits=[("act", rec.count("act"))],
                inc=("mm", 1))
        rec.emit("vector", lambda e: e.tensor_scalar(
            mrow[:], prs[:], 1.0 / D, None, ALU.mult),
            waits=[("mm", rec.count("mm"))], inc=("dve", 1))
        rec.emit("vector", lambda e: e.tensor_scalar(
            vrow[:], prq[:], 1.0 / D, None, ALU.mult),
            waits=[], inc=("dve", 1))
        rec.emit("vector", lambda e: e.tensor_mul(
            srow[:], mrow[:], mrow[:]),
            waits=[], inc=("dve", 1))
        rec.emit("vector", lambda e: e.tensor_sub(
            vrow[:], vrow[:], srow[:]),
            waits=[], inc=("dve", 1))
        rec.emit("scalar", lambda e: e.activation(
            srow[:], vrow[:], AF.Sqrt, bias=epsrow[:]),
            waits=[("dve", rec.count("dve"))], inc=("act", 1))
        rec.emit("vector", lambda e: e.reciprocal(
            rrow[:], srow[:]),
            waits=[("act", rec.count("act"))], inc=("dve", 1))
        rec.emit("tensor", lambda e: e.matmul(
            pbc[:], ones1x128[:], mrow[:], start=True, stop=True),
            waits=[("dve", rec.count("dve"))], inc=("mm", 1))
        rec.emit("scalar", lambda e, c0=c0: e.copy(
            meanB[:, c0:c0 + 512], pbc[:]),
            waits=[("mm", rec.count("mm"))], inc=("act", 1))
        rec.emit("tensor", lambda e: e.matmul(
            pbc[:], ones1x128[:], rrow[:], start=True, stop=True),
            waits=[("act", rec.count("act"))], inc=("mm", 1))
        rec.emit("scalar", lambda e, c0=c0: e.copy(
            rstdB[:, c0:c0 + 512], pbc[:]),
            waits=[("mm", rec.count("mm"))], inc=("act", 1))
        g["pbc_act"] = rec.count("act")

    if out_tiles is not None:
        for f in range(NF):
            rec.emit("vector", lambda e, f=f: e.tensor_sub(
                sq[f % 2][:], resid[f][:], meanB[:]),
                waits=(None if f < 2 else []), inc=("dve", 1))
            rec.emit("vector", lambda e, f=f: e.tensor_mul(
                out_tiles[f][:], sq[f % 2][:], rstdB[:]), waits=[], inc=("dve", 1))


def emit_out_proj(rec, g, w_ap, src_tiles, scale):
    wbuf = g["wbuf"]
    pipe = ProjPipe(rec, g["pproj"])
    ws = WStream(rec, wbuf)
    wcols = wbuf[0].ap().shape[2]

    def loader(cbase):
        def fn(e, wb):
            return e.dma_start(
                out=wb[:, :, 0:wcols],
                in_=w_ap[:, cbase:cbase + wcols].rearrange("(f p) c -> p f c", p=128))
        return fn

    def run_unit(cbase, wb, wld):
        for sub in range(wcols // 128):
            of = (cbase + sub * 128) // 128
            for half in range(2):
                def mk(f, pp, sub=sub, half=half, wb=wb):
                    return lambda e: e.matmul(
                        pp[:], wb[:, f, sub * 128:(sub + 1) * 128],
                        src_tiles[f][:, half * 512:(half + 1) * 512],
                        start=(f == 0), stop=(f == NF - 1))
                b, pp = pipe.group(mk, NF, wld=wld)
                rec.emit("vector", lambda e, of=of, half=half, pp=pp: e.scalar_tensor_tensor(
                    g["resid"][of][:, half * 512:(half + 1) * 512], pp[:], scale,
                    g["resid"][of][:, half * 512:(half + 1) * 512], ALU.mult, ALU.add),
                    waits=[("mm", rec.count("mm"))], inc=("dve", 1))
                pipe.done(b)

    prev = None
    for cbase in range(0, D, wcols):
        s, wb, wld = ws.load(loader(cbase))
        if prev is not None:
            run_unit(*prev[1:])
            ws.used(prev[0])
        prev = (s, cbase, wb, wld)
    run_unit(*prev[1:])
    ws.used(prev[0])


def emit_ffn(rec, g):
    resid = g["resid"]
    yT, wbuf = g["yT"], g["wbuf"]
    qb, kb, vb, ob = g["qb"], g["kb"], g["vb"], g["ob"]
    scr, cscr = g["fscr"], g["cscr"]
    scr2 = cscr
    zT = g["zT"]
    sc = g["fsc"]
    pproj, ptr = g["pproj"], g["ptr"]
    meanB, rstdB = g["meanB"], g["rstdB"]
    conv_s, cosb, sinb = g["conv_s"], g["cosb"], g["sinb"]
    rwb, rwrow = g["rwb"], g["rwrow"]
    epscol = g["epscol"]
    wf = g["w"]
    lnsq = g["lnsq"]

    pipe = ProjPipe(rec, g["pproj"])
    ws = WStream(rec, wbuf)
    g["pending_tr"] = None

    emit_ln(rec, g, None)
    if _FFN_LIMIT == "ln":
        return

    for f in range(NF):
        rec.emit("vector", lambda e, f=f: e.tensor_sub(
            lnsq[f % 2][:], resid[f][:], meanB[:]),
            waits=(None if f < 2 else [("act", rec.count("act"))]), inc=("dve", 1))
        rec.emit("vector", lambda e, f=f: e.tensor_mul(
            lnsq[f % 2][:], lnsq[f % 2][:], rstdB[:]), waits=[], inc=("dve", 1))
        rec.emit("scalar", lambda e, f=f: e.activation(
            yT[f][:, 0:512], lnsq[f % 2][:, 0:512], AF.Silu,
            scale=conv_s[:, f, 1:2]),
            waits=[("dve", rec.count("dve"))], inc=("act", 1))
        rec.emit("vector", lambda e, f=f: e.tensor_scalar(
            cscr[f % 2][:], lnsq[f % 2][:, 0:512], conv_s[:, f, 0:1],
            None, ALU.mult), waits=[], inc=("dve", 1))
        rec.emit("vector", lambda e, f=f: e.scalar_tensor_tensor(
            cscr[f % 2][:], lnsq[f % 2][:, 512:1024], conv_s[:, f, 1:2],
            cscr[f % 2][:], ALU.mult, ALU.add), waits=[], inc=("dve", 1))
        rec.emit("scalar", lambda e, f=f: e.activation(
            yT[f][:, 512:1024], cscr[f % 2][:], AF.Silu),
            waits=[("dve", rec.count("dve"))], inc=("act", 1))

    if _FFN_LIMIT == "conv":
        return
    pipe.floor()

    def V(fn, waits=()):
        rec.emit("vector", fn, waits=list(waits), inc=("dve", 1))

    for hg in range(NHG):
        qc0, vc0 = hg * 256, hg * 512

        rec.emit("gpsimd", lambda e, vc0=vc0: e.dma_start(
            out=rwrow[:], in_=wf["router_w"][:, vc0:vc0 + 512].unsqueeze(0)),
            waits=[("dve", rec.count("dve"))], inc=("wld", 16))
        for r in range(2):
            rec.emit("tensor", lambda e, r=r: e.matmul(
                g["pbc"][:], g["ones1x128"][:], rwrow[:, r, :],
                start=True, stop=True),
                waits=[("act", g.get("pbc_act", 0)), ("wld", rec.count("wld"))],
                inc=("mm", 1))
            rec.emit("scalar", lambda e, r=r: e.copy(
                rwb[:, r, 0:512], g["pbc"][:]),
                waits=[("mm", rec.count("mm"))], inc=("act", 1))
            g["pbc_act"] = rec.count("act")

        def unit_loader(wname, c0, n):
            def fn(e, wb, wname=wname, c0=c0, n=n):
                return e.dma_start(
                    out=wb[:, :, 0:n],
                    in_=wf[wname][:, c0:c0 + n].rearrange("(f p) c -> p f c", p=128))
            return fn

        sq_, wbq, wldq = ws.load(unit_loader("wq_f", qc0, 256))
        sk_, wbk, wldk = ws.load(unit_loader("wk_f", qc0, 256))
        for which in range(2):
            wb = wbq if which == 0 else wbk
            wld = wldq if which == 0 else wldk
            for tt in range(NTT):
                def mk(f, pp, tt=tt, wb=wb):
                    return lambda e: e.matmul(
                        pp[:, 0:256], yT[f][:, tt * 128:(tt + 1) * 128],
                        wb[:, f, 0:256], start=(f == 0), stop=(f == NF - 1))
                b, pp = pipe.group(mk, NF, wld=wld)
                if which == 0:
                    rec.emit("scalar", lambda e, tt=tt, pp=pp: e.activation(
                        qb[tt][:], pp[:, 0:256], AF.Silu),
                        waits=[("mm", rec.count("mm"))], inc=("act", 1))
                else:
                    rec.emit("scalar", lambda e, tt=tt, pp=pp: e.copy(
                        kb[tt][:], pp[:, 0:256]),
                        waits=[("mm", rec.count("mm"))], inc=("act", 1))
                pipe.done(b)
            ws.used(sq_ if which == 0 else sk_)

        if _FFN_LIMIT == ("qk", hg):
            return
        act_qk = rec.count("act")
        mm_tr = rec.count("mm")
        for dst in (qb, kb):
            for tt in range(4, 8):
                A, Bs = scr[tt % 2], ob[tt % 2]
                V(lambda e, dst=dst, tt=tt, A=A: e.tensor_tensor(
                    A[:, 0:256].rearrange("p (h j) -> p h j", j=64),
                    dst[tt][:].rearrange("p (h j) -> p h j", j=64),
                    cosb[:].unsqueeze(1).broadcast_to([128, 4, 64]), ALU.mult),
                    waits=[("act", act_qk), ("mm", mm_tr)])
                V(lambda e, dst=dst, tt=tt, Bs=Bs: e.tensor_tensor(
                    Bs[:, 0:256].rearrange("p (h j) -> p h j", j=64),
                    dst[tt][:].rearrange("p (h j) -> p h j", j=64),
                    sinb[:].unsqueeze(1).broadcast_to([128, 4, 64]), ALU.mult))

                def rot(e, dst=dst, tt=tt, A=A, Bs=Bs):
                    d3 = dst[tt][:].rearrange("p (h u j) -> p h u j", u=2, j=32)
                    a3 = A[:, 0:256].rearrange("p (h u j) -> p h u j", u=2, j=32)
                    b3 = Bs[:, 0:256].rearrange("p (h u j) -> p h u j", u=2, j=32)
                    e.tensor_sub(d3[:, :, 0, :], a3[:, :, 0, :], b3[:, :, 1, :])
                    return e.tensor_add(d3[:, :, 1, :], a3[:, :, 1, :], b3[:, :, 0, :])
                V(rot)

        if _FFN_LIMIT == ("rot", hg):
            return
        sv0, wbv0, _ = ws.load(unit_loader("wv_f", vc0, 256))
        sv1, wbv1, wldv = ws.load(unit_loader("wv_f", vc0 + 256, 256))
        if _FFN_LIMIT == ("vload", hg):
            return
        for tt in range(NTT):
            def mkv(f, pp, tt=tt, wbv0=wbv0, wbv1=wbv1):
                uu, fc = divmod(f, NF)
                wb = wbv0 if uu == 0 else wbv1
                return lambda e: e.matmul(
                    pp[:, uu * 256:(uu + 1) * 256],
                    yT[fc][:, tt * 128:(tt + 1) * 128], wb[:, fc, 0:256],
                    start=(fc == 0), stop=(fc == NF - 1))
            b, pp = pipe.group(mkv, 2 * NF, wld=wldv)
            if not _FFN_SKIP_VEVAC:
                rec.emit("scalar", lambda e, tt=tt, pp=pp: e.copy(
                    vb[tt][:], pp[:]),
                    waits=[("mm", rec.count("mm"))], inc=("act", 1))
            if not _FFN_SKIP_ROUTER:
                for r in range(2):
                    V(lambda e, tt=tt, pp=pp, r=r: e.tensor_tensor(
                        ob[tt % 2][:], pp[:], rwb[:, r, 0:512], ALU.mult),
                        waits=[("mm", rec.count("mm")), ("act", rec.count("act"))])
                    V(lambda e, tt=tt, r=r: e.reduce_sum(
                        sc["lg%d" % r][tt][:, 0:4],
                        ob[tt % 2][:].rearrange("p (h w) -> p h w", w=128), axis=AXX))
            pipe.done(b)
        ws.used(sv0)
        ws.used(sv1)

        if _FFN_LIMIT == ("v", hg):
            return
        for j in range(4):
            for nm, qi, ki in (("d00", j, j), ("d10", j + 4, j), ("d11", j + 4, j + 4)):
                V(lambda e, qi=qi, ki=ki, j=j: e.tensor_tensor(
                    scr[j % 2][:, 0:256], qb[qi][:], kb[ki][:], ALU.mult))
                V(lambda e, nm=nm, j=j: e.reduce_sum(
                    sc[nm][j][:, 0:4],
                    scr[j % 2][:, 0:256].rearrange("p (h w) -> p h w", w=64), axis=AXX))

        if _FFN_LIMIT == ("dots", hg):
            return
        for j in range(4):
            t0, t1 = j, j + 4
            for tt in (t0, t1):
                V(lambda e, tt=tt: e.tensor_sub(
                    sc["da"][tt][:, 0:4], sc["lg0"][tt][:, 0:4], sc["lg1"][tt][:, 0:4]))
                rec.emit("scalar", lambda e, tt=tt: e.activation(
                    sc["s0"][tt][:, 0:4], sc["da"][tt][:, 0:4], AF.Sigmoid),
                    waits=[("dve", rec.count("dve"))], inc=("act", 1))
                rec.emit("scalar", lambda e, tt=tt: e.activation(
                    sc["s1"][tt][:, 0:4], sc["da"][tt][:, 0:4], AF.Sigmoid, scale=-1.0),
                    waits=[], inc=("act", 1))
                V(lambda e, tt=tt: e.tensor_single_scalar(
                    sc["oh0"][tt][:, 0:4], sc["da"][tt][:, 0:4], 0.0, ALU.is_ge))
                V(lambda e, tt=tt: e.tensor_scalar(
                    sc["oh1"][tt][:, 0:4], sc["oh0"][tt][:, 0:4], -1.0, 1.0,
                    ALU.mult, ALU.add))
            act_sg = rec.count("act")
            for tt in (t0, t1):
                V(lambda e, tt=tt: e.tensor_mul(
                    sc["ms0"][tt][:, 0:4], sc["s0"][tt][:, 0:4], sc["oh0"][tt][:, 0:4]),
                    waits=[("act", act_sg)])
                V(lambda e, tt=tt: e.tensor_mul(
                    sc["ms1"][tt][:, 0:4], sc["s1"][tt][:, 0:4], sc["oh1"][tt][:, 0:4]))
            V(lambda e, t0=t0: e.tensor_add(
                sc["w00"][t0][:, 0:4], sc["ms0"][t0][:, 0:4], sc["ms1"][t0][:, 0:4]))
            for r in ("0", "1"):
                V(lambda e, r=r, t0=t0, t1=t1: e.tensor_add(
                    sc["n" + r][t1][:, 0:4], sc["oh" + r][t0][:, 0:4],
                    sc["oh" + r][t1][:, 0:4]))
                V(lambda e, r=r, t1=t1: e.tensor_scalar_max(
                    sc["t1s"][t1][:, 0:4], sc["n" + r][t1][:, 0:4], 1.0))
                V(lambda e, r=r, t1=t1: e.reciprocal(
                    sc["t2s"][t1][:, 0:4], sc["t1s"][t1][:, 0:4]))
                V(lambda e, r=r, t1=t1: e.tensor_single_scalar(
                    sc["t1s"][t1][:, 0:4], sc["n" + r][t1][:, 0:4], 0.5, ALU.is_ge))
                V(lambda e, r=r, t1=t1: e.tensor_mul(
                    sc["n" + r][t1][:, 0:4], sc["t1s"][t1][:, 0:4], sc["t2s"][t1][:, 0:4]))
                V(lambda e, r=r, t1=t1: e.tensor_mul(
                    sc["f" + r][t1][:, 0:4], sc["ms" + r][t1][:, 0:4],
                    sc["n" + r][t1][:, 0:4]))
            V(lambda e, t0=t0, t1=t1: e.tensor_mul(
                sc["t1s"][t0][:, 0:4], sc["f0"][t1][:, 0:4], sc["oh0"][t0][:, 0:4]))
            V(lambda e, t0=t0, t1=t1: e.tensor_mul(
                sc["t2s"][t0][:, 0:4], sc["f1"][t1][:, 0:4], sc["oh1"][t0][:, 0:4]))
            V(lambda e, t0=t0: e.tensor_add(
                sc["w10"][t0][:, 0:4], sc["t1s"][t0][:, 0:4], sc["t2s"][t0][:, 0:4]))
            V(lambda e, t0=t0, t1=t1: e.tensor_add(
                sc["w11"][t0][:, 0:4], sc["f0"][t1][:, 0:4], sc["f1"][t1][:, 0:4]))
            for nm in ("00", "10", "11"):
                V(lambda e, nm=nm, t0=t0: e.scalar_tensor_tensor(
                    sc["qk" + nm][t0][:, 0:4], sc["d" + nm][t0][:, 0:4], FFN_SCALE,
                    sc["w" + nm][t0][:, 0:4], ALU.mult, ALU.mult))

        if _FFN_LIMIT == ("scal", hg):
            return
        if g.get("pending_tr") is not None:
            g["pending_tr"]()
            g["pending_tr"] = None
        sg0, wbg0, _ = ws.load(unit_loader("wg_f", vc0, 256))
        sg1, wbg1, wldg = ws.load(unit_loader("wg_f", vc0 + 256, 256))
        zst = g["zst"]
        mm_tr_prev = rec.count("mm")
        for tt in range(NTT):
            def mkg(f, pp, tt=tt, wbg0=wbg0, wbg1=wbg1):
                uu, fc = divmod(f, NF)
                wb = wbg0 if uu == 0 else wbg1
                return lambda e: e.matmul(
                    pp[:, uu * 256:(uu + 1) * 256],
                    yT[fc][:, tt * 128:(tt + 1) * 128], wb[:, fc, 0:256],
                    start=(fc == 0), stop=(fc == NF - 1))
            b, pp = pipe.group(mkg, 2 * NF, wld=wldg)
            rec.emit("scalar", lambda e, tt=tt, pp=pp: e.activation(
                zst[tt][:], pp[:], AF.Silu),
                waits=[("mm", rec.count("mm")), ("dve", rec.count("dve"))],
                inc=("act", 1))
            pipe.done(b)
        ws.used(sg0)
        ws.used(sg1)

        def bc(nm, tt):
            return sc[nm][tt][:, 0:4].unsqueeze(2).broadcast_to([128, 4, 128])

        act_g = rec.count("act")
        zmul_counts = {}
        for j in range(4):
            t0, t1 = j, j + 4
            V(lambda e, t0=t0: e.tensor_tensor(
                ob[0][:].rearrange("p (h w) -> p h w", w=128),
                bc("qk00", t0), vb[t0][:].rearrange("p (h w) -> p h w", w=128), ALU.mult),
                waits=[("act", rec.count("act"))])
            V(lambda e, t0=t0: e.tensor_tensor(
                scr[0][:].rearrange("p (h w) -> p h w", w=128),
                bc("qk10", t0), vb[t0][:].rearrange("p (h w) -> p h w", w=128), ALU.mult))
            V(lambda e, t0=t0, t1=t1: e.tensor_tensor(
                scr[1][:].rearrange("p (h w) -> p h w", w=128),
                bc("qk11", t0), vb[t1][:].rearrange("p (h w) -> p h w", w=128), ALU.mult))
            V(lambda e: e.tensor_add(ob[1][:], scr[0][:], scr[1][:]))

            for oi, tt in ((0, t0), (1, t1)):
                o3 = ob[oi][:].rearrange("p (h w) -> p h w", w=128)
                V(lambda e, tt=tt, o3=o3: e.reduce_sum(sc["mu"][tt][:, 0:4], o3, axis=AXX))
                V(lambda e, oi=oi: e.tensor_mul(scr[1][:], ob[oi][:], ob[oi][:]))
                V(lambda e, tt=tt: e.reduce_sum(
                    sc["vv"][tt][:, 0:4],
                    scr[1][:].rearrange("p (h w) -> p h w", w=128), axis=AXX))
                V(lambda e, tt=tt: e.tensor_scalar(
                    sc["mu"][tt][:, 0:4], sc["mu"][tt][:, 0:4], 1.0 / 128, None, ALU.mult))
                V(lambda e, tt=tt: e.tensor_scalar(
                    sc["vv"][tt][:, 0:4], sc["vv"][tt][:, 0:4], 1.0 / 128, None, ALU.mult))
                V(lambda e, tt=tt: e.tensor_mul(
                    sc["t1s"][tt][:, 0:4], sc["mu"][tt][:, 0:4], sc["mu"][tt][:, 0:4]))
                V(lambda e, tt=tt: e.tensor_sub(
                    sc["vv"][tt][:, 0:4], sc["vv"][tt][:, 0:4], sc["t1s"][tt][:, 0:4]))
                rec.emit("scalar", lambda e, tt=tt: e.activation(
                    sc["t1s"][tt][:, 0:4], sc["vv"][tt][:, 0:4], AF.Sqrt, bias=epscol[:]),
                    waits=[("dve", rec.count("dve"))], inc=("act", 1))
                rec.emit("vector", lambda e, tt=tt: e.reciprocal(
                    sc["rs"][tt][:, 0:4], sc["t1s"][tt][:, 0:4]),
                    waits=[("act", rec.count("act"))], inc=("dve", 1))
                V(lambda e, tt=tt, o3=o3: e.tensor_tensor(o3, o3, bc("mu", tt), ALU.subtract))
                V(lambda e, tt=tt, o3=o3: e.tensor_tensor(o3, o3, bc("rs", tt), ALU.mult))
                V(lambda e, tt=tt, oi=oi: e.tensor_mul(
                    zst[tt][:], zst[tt][:], ob[oi][:]), waits=[("act", act_g)])
                zmul_counts[tt] = rec.count("dve")

        def make_tr(hg=hg, zc=dict(zmul_counts)):
            def emit_tr():
                for tt in range(NTT):
                    for c in range(4):
                        rec.emit("tensor", lambda e, tt=tt, c=c: e.transpose(
                            ptr[:, c * 128:(c + 1) * 128],
                            zst[tt][:, c * 128:(c + 1) * 128], g["identb"][:]),
                            waits=([("dve", max(zc[tt], rec.count("dve")))]
                                   if c == 0 else []),
                            inc=(("mm", 1) if c == 3 else None))

                    def evac_tr(e, tt=tt, hg=hg):
                        last = None
                        for c in range(4):
                            last = e.tensor_copy(zT[hg * 4 + c][:, tt * 128:(tt + 1) * 128],
                                                 ptr[:, c * 128:(c + 1) * 128])
                        return last
                    rec.emit("vector", evac_tr, waits=[("mm", rec.count("mm"))],
                             inc=("dve", 1))
            return emit_tr
        g["pending_tr"] = make_tr()

    if g["pending_tr"] is not None:
        g["pending_tr"]()
        g["pending_tr"] = None
        emit_out_proj(rec, g, wf["wout_f"], zT, 0.5)


def emit_attn_proj(rec, g):
    wf = g["w"]
    xT, wbuf = g["xT"], g["wbuf"]
    qT, kst, vst = g["qT"], g["kst"], g["vst"]
    sqb = g["sqb"]
    prow = g["prs"]
    onesb, onesf32, ones1x128 = g["onesb"], g["onesf32"], g["ones1x128"]

    emit_ln(rec, g, out_tiles=xT)
    pipe = ProjPipe(rec, g["pproj"])
    ws = WStream(rec, wbuf)

    def unit_loader(wname, c0, n):
        def fn(e, wb, wname=wname, c0=c0, n=n):
            return e.dma_start(
                out=wb[:, :, 0:n],
                in_=wf[wname][:, c0:c0 + n].rearrange("(f p) c -> p f c", p=128))
        return fn

    def emit_bound(buf, nm):
        for half in range(2):
            rec.emit("tensor", lambda e, buf=buf, half=half: e.matmul(
                prow[:], onesb[:], buf[:, half * 512:(half + 1) * 512],
                start=True, stop=True),
                waits=[("dve", rec.count("dve"))], inc=("mm", 1))
            rec.emit("vector", lambda e: e.reduce_max(
                g["rmax"][:], prow[:], axis=AXX),
                waits=[("mm", rec.count("mm"))], inc=("dve", 1))
            rec.emit("vector", lambda e, nm=nm: e.tensor_max(
                g[nm][:], g[nm][:], g["rmax"][:]), waits=[], inc=("dve", 1))

    pending_bound = None
    for h in range(H):
        s_, wb, wld = ws.load(unit_loader("wk_a", h * 128, 128))
        dst = kst[h % 2]
        for half in range(2):
            def mk(f, pp, half=half, wb=wb):
                return lambda e: e.matmul(
                    pp[:], wb[:, f, 0:128],
                    xT[f][:, half * 512:(half + 1) * 512],
                    start=(f == 0), stop=(f == NF - 1))
            b, pp = pipe.group(mk, NF, wld=wld)
            rec.emit("scalar", lambda e, dst=dst, half=half, pp=pp: e.copy(
                dst[:, half * 512:(half + 1) * 512], pp[:]),
                waits=[("mm", rec.count("mm")), ("sld", rec.count("sld")),
                       ("dve", rec.count("dve"))],
                inc=("act", 1))
            pipe.done(b)
        ws.used(s_)
        buf = sqb[h % 2]
        rec.emit("vector", lambda e, dst=dst, buf=buf: e.tensor_mul(
            buf[:], dst[:], dst[:]),
            waits=[("act", rec.count("act")), ("mm", rec.count("mm"))],
            inc=("dve", 1))
        if pending_bound is not None:
            pending_bound()
        pending_bound = (lambda buf=buf: emit_bound(buf[:], "kmax2"))
        rec.emit("sync", lambda e, h=h, dst=dst: e.dma_start(
            out=g["k_sh"].ap()[h * 128:(h + 1) * 128, :], in_=dst[:]),
            waits=[("act", rec.count("act"))], inc=("sld", 16))
    pending_bound()
    pending_bound = None

    for oc in range(8):
        s_, wb, wld = ws.load(unit_loader("wv_a", oc * 256, 256))
        for tt in range(NTT):
            def mkv(f, pp, tt=tt, wb=wb):
                return lambda e: e.matmul(
                    pp[:, 0:256], xT[f][:, tt * 128:(tt + 1) * 128],
                    wb[:, f, 0:256], start=(f == 0), stop=(f == NF - 1))
            b, pp = pipe.group(mkv, NF, wld=wld)
            rec.emit("scalar", lambda e, tt=tt, pp=pp: e.copy(
                vst[tt % 2][:, 0:256], pp[:, 0:256]),
                waits=[("mm", rec.count("mm")), ("sld", rec.count("sld"))],
                inc=("act", 1))
            rec.emit("sync", lambda e, tt=tt, oc=oc: e.dma_start(
                out=g["v_sh"].ap()[tt * 128:(tt + 1) * 128, oc * 256:(oc + 1) * 256],
                in_=vst[tt % 2][:, 0:256]),
                waits=[("act", rec.count("act"))], inc=("sld", 16))
            pipe.done(b)
        ws.used(s_)

    rec.emit("gpsimd", lambda e: e.collective_compute(
        "AllGather", ALU.bypass, replica_groups=[list(range(NCORES))],
        ins=[g["k_sh"].ap().opt()], outs=[g["k_all"].ap().opt()]),
        waits=[("sld", rec.count("sld"))], inc=("cc", 1))
    rec.emit("gpsimd", lambda e: e.collective_compute(
        "AllGather", ALU.bypass, replica_groups=[list(range(NCORES))],
        ins=[g["v_sh"].ap().opt()], outs=[g["v_all"].ap().opt()]),
        waits=[], inc=("cc", 1))
    g["cc_after_gather"] = rec.count("cc")

    rec.emit("gpsimd", lambda e: None, waits=[("cc", rec.count("cc"))])
    for h in range(H):
        s_, wb, wld = ws.load(unit_loader("wq_a", h * 128, 128))
        for half in range(2):
            def mk(f, pp, half=half, wb=wb):
                return lambda e: e.matmul(
                    pp[:], wb[:, f, 0:128],
                    xT[f][:, half * 512:(half + 1) * 512],
                    start=(f == 0), stop=(f == NF - 1))
            b, pp = pipe.group(mk, NF, wld=wld)
            rec.emit("scalar", lambda e, h=h, half=half, pp=pp: e.copy(
                qT[h][:, half * 512:(half + 1) * 512], pp[:]),
                waits=[("mm", rec.count("mm")), ("dve", rec.count("dve"))],
                inc=("act", 1))
            pipe.done(b)
        ws.used(s_)
        buf = sqb[h % 2]
        rec.emit("vector", lambda e, h=h, buf=buf: e.tensor_mul(
            buf[:], qT[h][:], qT[h][:]),
            waits=[("act", rec.count("act")), ("mm", rec.count("mm"))],
            inc=("dve", 1))
        if pending_bound is not None:
            pending_bound()
        pending_bound = (lambda buf=buf: emit_bound(buf[:], "qmax2"))
    pending_bound()

    rec.emit("vector", lambda e: e.tensor_copy(g["mxrow"][:, 0:1], g["qmax2"][:]),
             inc=("dve", 1))
    rec.emit("vector", lambda e: e.tensor_copy(g["mxrow"][:, 1:2], g["kmax2"][:]),
             waits=[], inc=("dve", 1))
    rec.emit("sync", lambda e: e.dma_start(out=g["mx_sh"].ap(), in_=g["mxrow"][:]),
             waits=[("dve", rec.count("dve"))], inc=("sld", 16))
    rec.emit("gpsimd", lambda e: e.collective_compute(
        "AllReduce", ALU.max, replica_groups=[list(range(NCORES))],
        ins=[g["mx_sh"].ap().opt()], outs=[g["mx_all"].ap().opt()]),
        waits=[("sld", rec.count("sld"))], inc=("cc", 1))
    rec.emit("sync", lambda e: e.dma_start(out=g["mxrow"][:], in_=g["mx_all"].ap()),
             waits=[("cc", rec.count("cc"))], inc=("sld", 16))
    rec.emit("scalar", lambda e: e.activation(g["mxrow"][:], g["mxrow"][:], AF.Sqrt),
             waits=[("sld", rec.count("sld"))], inc=("act", 1))
    rec.emit("vector", lambda e: e.tensor_mul(
        g["negm"][:], g["mxrow"][:, 0:1], g["mxrow"][:, 1:2]),
        waits=[("act", rec.count("act"))], inc=("dve", 1))
    rec.emit("vector", lambda e: e.tensor_scalar(
        g["negm"][:], g["negm"][:], -ATTN_SCALE, None, ALU.mult),
        waits=[], inc=("dve", 1))
    rec.emit("tensor", lambda e: e.matmul(
        g["pbc"][:, 0:1], ones1x128[:], g["negm"][:], start=True, stop=True),
        waits=[("dve", rec.count("dve"))], inc=("mm", 1))
    rec.emit("scalar", lambda e: e.copy(g["negMcol"][:], g["pbc"][:, 0:1]),
             waits=[("mm", rec.count("mm"))], inc=("act", 1))


def emit_attn_core(rec, g):
    qT, OnT = g["qT"], g["OnT"]
    kbh, vbh, ptile = g["kbh"], g["vbh"], g["ptile"]
    sp, pO1, pO2, zb, pf = g["sp"], g["pO1"], g["pO2"], g["zb"], g["pf"]
    onesb, onesf32, ones1x128 = g["onesb"], g["onesf32"], g["ones1x128"]
    oc1, oc2 = g["oc1"], g["oc2"]
    kall = g["k_all"].ap().rearrange("(r hh p) t -> p hh r t", r=NCORES, p=128)
    vall = g["v_all"].ap().rearrange("(r bb plc p) c -> p bb r plc c",
                                     r=NCORES, bb=2, plc=4, p=128)
    base_smm = rec.count("smm")
    base_sxp = rec.count("sxp")
    base_zav = rec.count("zav")
    act_pre = rec.count("act")
    sld_base = rec.count("sld")

    KV_DMAS = 1 + NCORES

    def kv_load(bh):
        b, h = bh // H, bh % H
        rec.emit("sync", lambda e, b=b, h=h, bh=bh: e.dma_start(
            out=kbh[bh % 2][:].rearrange("p (r t) -> p r t", r=NCORES),
            in_=kall[:, h, :, b * 512:(b + 1) * 512]),
            waits=([("cc", g["cc_after_gather"])] if bh < 2 else
                   [("zav", base_zav + NKT * (bh - 1))]),
            inc=("sld", 16))
        for r in range(NCORES):
            rec.emit("sync", lambda e, b=b, h=h, bh=bh, r=r: e.dma_start(
                out=vbh[bh % 2][:, r * 512:(r + 1) * 512]
                .rearrange("p (plc c) -> p plc c", c=128),
                in_=vall[:, b, r, :, h * 128:(h + 1) * 128]),
                waits=[], inc=("sld", 16))

    kv_load(0)
    kv_load(1)
    emit_finalize = emit_finalize_mk(rec, g)

    NT_ALL = 2 * H * NKT

    def s_wait(gt):
        bh, t = divmod(gt, NKT)
        w = [("sxp", base_sxp + gt - 1)] if gt >= 1 else [("act", act_pre)]
        if t == 0:
            sld_need = sld_base + 16 * KV_DMAS * min(bh + 1, 2)
            if bh >= 2:
                sld_need = sld_base + 16 * KV_DMAS * (bh + 1)
            w = w + [("sld", sld_need)]
        return w

    def emit_s(gt):
        bh, t = divmod(gt, NKT)
        kv = kbh[bh % 2]

        def s_pair(e, kv=kv, h=bh % H, b=bh // H, t=t):
            e.matmul(sp[t % 2][:, 0:512], kv[0:64, t * 128:(t + 1) * 128],
                     qT[h][0:64, b * 512:(b + 1) * 512],
                     start=True, stop=True, tile_position=(0, 0))
            return e.matmul(sp[t % 2][:, 512:1024],
                            kv[64:128, t * 128:(t + 1) * 128],
                            qT[h][64:128, b * 512:(b + 1) * 512],
                            start=True, stop=True, tile_position=(64, 0))
        rec.emit("tensor", s_pair, waits=(None if _ATTN_SERIAL else s_wait(gt)),
                 inc=("smm", 1))

    emit_s(0)
    emit_s(1)
    dve_fin_prev = rec.count("dve")
    for gt in range(NT_ALL):
        bh, t = divmod(gt, NKT)
        b, h = bh // H, bh % H
        vv = vbh[bh % 2]
        if t == 0:
            if 2 <= bh + 1 < 2 * H:
                kv_load(bh + 1)
        if gt + 2 < NT_ALL:
            emit_s(gt + 2)

        w_e = [("smm", base_smm + gt + 1)]
        if gt >= 2:
            w_e.append(("zav", base_zav + gt - 1))
        rec.emit("scalar", lambda e, t=t: e.activation(
            ptile[t % 2][:], sp[t % 2][:], AF.Exp,
            bias=g["negMcol"][:], scale=ATTN_SCALE),
            waits=(None if _ATTN_SERIAL else w_e), inc=("sxp", 1))

        w_z = [("sxp", base_sxp + gt + 1)]
        if t == 0:
            w_z.append(("dve", dve_fin_prev))

        def z_pair(e, t=t):
            e.matmul(zb[0:1, :], onesb[:], ptile[t % 2][:, 0:512],
                     start=(t == 0), stop=(t == NKT - 1), tile_position=(0, 0))
            return e.matmul(zb[32:33, :], onesb[:], ptile[t % 2][:, 512:1024],
                            start=(t == 0), stop=(t == NKT - 1),
                            tile_position=(0, 32))
        rec.emit("tensor", z_pair, waits=(None if _ATTN_SERIAL else w_z), inc=None)

        def av_pair(e, t=t, vv=vv):
            e.matmul(pO1[:], vv[:, t * 128:(t + 1) * 128], ptile[t % 2][:, 0:512],
                     start=(t == 0), stop=(t == NKT - 1))
            return e.matmul(pO2[:], vv[:, t * 128:(t + 1) * 128],
                            ptile[t % 2][:, 512:1024],
                            start=(t == 0), stop=(t == NKT - 1))
        rec.emit("tensor", av_pair, waits=[], inc=("zav", 1))

        if t == NKT - 1:
            emit_finalize(bh)
            dve_fin_prev = rec.count("dve")


def emit_finalize_mk(rec, g):
    OnT = g["OnT"]
    zb, pf, pO1, pO2 = g["zb"], g["pf"], g["pO1"], g["pO2"]
    onesf32, ones1x128 = g["onesf32"], g["ones1x128"]
    oc1, oc2 = g["oc1"], g["oc2"]

    def emit_finalize(bh):
        b, h = bh // H, bh % H
        zav_done = [("zav", rec.count("zav"))]
        rec.emit("vector", lambda e: e.tensor_copy(g["z1r"][:], zb[0:1, :]),
                 waits=zav_done, inc=("dve", 1))
        rec.emit("vector", lambda e: e.tensor_copy(g["z2r"][:], zb[32:33, :]),
                 waits=[], inc=("dve", 1))
        rec.emit("vector", lambda e: e.reciprocal(g["ar"][:], g["z1r"][:]),
                 waits=[], inc=("dve", 1))
        rec.emit("vector", lambda e: e.reciprocal(g["br"][:], g["z2r"][:]),
                 waits=[], inc=("dve", 1))
        rec.emit("vector", lambda e: e.tensor_scalar(
            g["br"][:], g["br"][:], g["lamv"][:], None, ALU.mult),
            waits=[], inc=("dve", 1))
        rec.emit("tensor", lambda e: e.matmul(
            pf[:], ones1x128[:], g["ar"][:], start=True, stop=True),
            waits=[("dve", rec.count("dve"))], inc=("mm", 1))
        rec.emit("scalar", lambda e: e.copy(g["BaS"][:], pf[:]),
                 waits=[("mm", rec.count("mm"))], inc=("act", 1))
        rec.emit("tensor", lambda e: e.matmul(
            pf[:], ones1x128[:], g["br"][:], start=True, stop=True),
            waits=[("act", rec.count("act"))], inc=("mm", 1))
        rec.emit("scalar", lambda e: e.copy(g["BbS"][:], pf[:]),
                 waits=[("mm", rec.count("mm"))], inc=("act", 1))
        rec.emit("vector", lambda e: e.tensor_mul(oc1[:], pO1[:], g["BaS"][:]),
                 waits=[("act", rec.count("act"))], inc=("dve", 1))
        rec.emit("vector", lambda e: e.tensor_mul(oc2[:], pO2[:], g["BbS"][:]),
                 waits=[], inc=("dve", 1))
        rec.emit("vector", lambda e: e.tensor_sub(oc1[:], oc1[:], oc2[:]),
                 waits=[], inc=("dve", 1))
        rec.emit("tensor", lambda e: e.matmul(
            zb[0:1, :], onesf32[:], oc1[:], start=True, stop=True),
            waits=[("dve", rec.count("dve"))], inc=("mm", 1))
        rec.emit("vector", lambda e: e.tensor_mul(oc2[:], oc1[:], oc1[:]),
                 waits=[], inc=("dve", 1))
        rec.emit("tensor", lambda e: e.matmul(
            zb[32:33, :], onesf32[:], oc2[:], start=True, stop=True),
            waits=[("dve", rec.count("dve"))], inc=("mm", 1))
        rec.emit("vector", lambda e: e.tensor_scalar(
            g["z1r"][:], zb[0:1, :], 1.0 / 128, None, ALU.mult),
            waits=[("mm", rec.count("mm"))], inc=("dve", 1))
        rec.emit("vector", lambda e: e.tensor_scalar(
            g["z2r"][:], zb[32:33, :], 1.0 / 128, None, ALU.mult),
            waits=[], inc=("dve", 1))
        rec.emit("vector", lambda e: e.tensor_mul(g["ar"][:], g["z1r"][:], g["z1r"][:]),
                 waits=[], inc=("dve", 1))
        rec.emit("vector", lambda e: e.tensor_sub(g["z2r"][:], g["z2r"][:], g["ar"][:]),
                 waits=[], inc=("dve", 1))
        rec.emit("scalar", lambda e: e.activation(
            g["br"][:], g["z2r"][:], AF.Sqrt, bias=g["epsrow"][:]),
            waits=[("dve", rec.count("dve"))], inc=("act", 1))
        rec.emit("vector", lambda e: e.reciprocal(g["ar"][:], g["br"][:]),
                 waits=[("act", rec.count("act"))], inc=("dve", 1))
        rec.emit("tensor", lambda e: e.matmul(
            pf[:], ones1x128[:], g["z1r"][:], start=True, stop=True),
            waits=[("dve", rec.count("dve"))], inc=("mm", 1))
        rec.emit("vector", lambda e: e.tensor_sub(oc1[:], oc1[:], pf[:]),
                 waits=[("mm", rec.count("mm"))], inc=("dve", 1))
        rec.emit("tensor", lambda e: e.matmul(
            pf[:], ones1x128[:], g["ar"][:], start=True, stop=True),
            waits=[("dve", rec.count("dve"))], inc=("mm", 1))
        rec.emit("vector", lambda e: e.tensor_mul(oc1[:], oc1[:], pf[:]),
                 waits=[("mm", rec.count("mm"))], inc=("dve", 1))
        rec.emit("vector", lambda e, h=h, b=b: e.tensor_scalar(
            OnT[h][:, b * 512:(b + 1) * 512], oc1[:],
            g["gnw_s"][:, h:h + 1], g["gnb_s"][:, h:h + 1], ALU.mult, ALU.add),
            waits=[], inc=("dve", 1))

    return emit_finalize


def emit_setup(rec, g):
    wf = g["w"]
    for nm, val in (("onesf32", 1.0), ("ones1x128", 1.0), ("onesb", 1.0),
                    ("epsrow", LN_EPS), ("epscol", LN_EPS),
                    ("qmax2", 0.0), ("kmax2", 0.0), ("pihalf", np.pi / 2)):
        rec.emit("vector", lambda e, nm=nm, val=val: e.memset(g[nm][:], val),
                 waits=[], inc=("dve", 1))
    for f in range(NF):
        rec.emit("sync", lambda e, f=f: e.dma_start(
            out=g["resid"][f][:], in_=g["x_d"][f * 128:(f + 1) * 128, :]),
            waits=[], inc=("sld", 16))
    rec.emit("sync", lambda e: e.dma_start(out=g["identb"][:], in_=wf["ident"]),
             waits=[], inc=("sld", 16))
    rec.emit("gpsimd", lambda e: e.dma_start(
        out=g["conv_s"][:], in_=wf["conv_w"].rearrange("(f p) t -> p f t", p=128)),
        waits=[], inc=("wld", 16))
    rec.emit("gpsimd", lambda e: e.dma_start(
        out=g["gnw_s"][:], in_=wf["gnw"]), waits=[], inc=("wld", 16))
    rec.emit("gpsimd", lambda e: e.dma_start(
        out=g["gnb_s"][:], in_=wf["gnb"]), waits=[], inc=("wld", 16))
    for i, nm in enumerate(("lq1", "lk1", "lq2", "lk2")):
        rec.emit("gpsimd", lambda e, i=i, nm=nm: e.dma_start(
            out=g["lrows"][:, i * 64:(i + 1) * 64], in_=wf[nm]),
            waits=[], inc=("wld", 16))
    rec.emit("vector", lambda e: e.tensor_scalar(
        g["gnw_s"][:], g["gnw_s"][:], 1.0 - LAMBDA_INIT, None, ALU.mult),
        waits=[("wld", rec.count("wld"))], inc=("dve", 1))
    rec.emit("vector", lambda e: e.tensor_scalar(
        g["gnb_s"][:], g["gnb_s"][:], 1.0 - LAMBDA_INIT, None, ALU.mult),
        waits=[], inc=("dve", 1))
    rec.emit("vector", lambda e: e.tensor_mul(
        g["lscr"][:, 0:64], g["lrows"][:, 0:64], g["lrows"][:, 64:128]),
        waits=[], inc=("dve", 1))
    rec.emit("vector", lambda e: e.tensor_mul(
        g["lscr"][:, 64:128], g["lrows"][:, 128:192], g["lrows"][:, 192:256]),
        waits=[], inc=("dve", 1))
    rec.emit("vector", lambda e: e.reduce_sum(
        g["lsum"][:, 0:1], g["lscr"][:, 0:64].rearrange("p (o w) -> p o w", o=1),
        axis=AXX), waits=[], inc=("dve", 1))
    rec.emit("vector", lambda e: e.reduce_sum(
        g["lsum"][:, 1:2], g["lscr"][:, 64:128].rearrange("p (o w) -> p o w", o=1),
        axis=AXX), waits=[], inc=("dve", 1))
    rec.emit("scalar", lambda e: e.activation(g["lsum"][:], g["lsum"][:], AF.Exp),
             waits=[("dve", rec.count("dve"))], inc=("act", 1))
    rec.emit("vector", lambda e: e.tensor_sub(
        g["lamv"][:], g["lsum"][:, 0:1], g["lsum"][:, 1:2]),
        waits=[("act", rec.count("act"))], inc=("dve", 1))
    rec.emit("vector", lambda e: e.tensor_scalar(
        g["lamv"][:], g["lamv"][:], 1.0, LAMBDA_INIT, ALU.mult, ALU.add),
        waits=[], inc=("dve", 1))
    rec.emit("gpsimd", lambda e: e.iota(
        g["iotar"][:], [[1, 32]], base=0, channel_multiplier=0,
        allow_small_or_imprecise_dtypes=True), waits=[], inc=("cc", 1))
    rec.emit("scalar", lambda e: e.activation(
        g["iotar"][:], g["iotar"][:], AF.Exp, scale=-float(np.log(10000.0) / 32.0)),
        waits=[("cc", rec.count("cc"))], inc=("act", 1))
    rec.emit("scalar", lambda e: e.activation(
        g["trig"][:, 0:32], g["iotar"][:], AF.Sin), waits=[], inc=("act", 1))
    rec.emit("scalar", lambda e: e.activation(
        g["trig"][:, 32:64], g["iotar"][:], AF.Sin), waits=[], inc=("act", 1))
    rec.emit("scalar", lambda e: e.activation(
        g["trig"][:, 64:96], g["iotar"][:], AF.Sin, bias=g["pihalf"][:]),
        waits=[("dve", rec.count("dve"))], inc=("act", 1))
    rec.emit("scalar", lambda e: e.activation(
        g["trig"][:, 96:128], g["iotar"][:], AF.Sin, bias=g["pihalf"][:]),
        waits=[], inc=("act", 1))
    rec.emit("tensor", lambda e: e.matmul(
        g["pbc"][:, 0:64], g["ones1x128"][:], g["trig"][:, 64:128],
        start=True, stop=True),
        waits=[("act", rec.count("act")), ("dve", rec.count("dve"))], inc=("mm", 1))
    rec.emit("scalar", lambda e: e.copy(g["cosb"][:], g["pbc"][:, 0:64]),
             waits=[("mm", rec.count("mm"))], inc=("act", 1))
    rec.emit("tensor", lambda e: e.matmul(
        g["pbc"][:, 0:64], g["ones1x128"][:], g["trig"][:, 0:64],
        start=True, stop=True),
        waits=[("act", rec.count("act"))], inc=("mm", 1))
    rec.emit("scalar", lambda e: e.copy(g["sinb"][:], g["pbc"][:, 0:64]),
             waits=[("mm", rec.count("mm"))], inc=("act", 1))


def emit_dumps(rec, g, nc, tag):
    for nm in _DUMPS:
        if nm.startswith("fsc:"):
            obj = g["fsc"].get(nm[4:])
        elif nm.startswith("dram:"):
            _, base, r0s, r1s = nm.split(":")
            src_ap = g[base].ap()[int(r0s):int(r1s), :]
            dd = nc.dram_tensor("dump_%s%s_%s" % (tag, base, r0s), list(src_ap.shape),
                                src_ap.dtype, kind="ExternalOutput").ap()
            rec.emit("sync", lambda e, src_ap=src_ap, dd=dd: e.dma_start(
                out=dd, in_=src_ap), inc=("sld", 16))
            continue
        else:
            obj = g.get(nm)
        if obj is None:
            continue
        nm = nm.replace(":", "_")
        tiles = obj if isinstance(obj, list) else [obj]
        aps = [t[:] if hasattr(t, "ap") or hasattr(t, "__enter__") or True else t
               for t in tiles]
        aps = []
        for t in tiles:
            try:
                aps.append(t[:])
            except TypeError:
                aps.append(t)
        rows = sum(a.shape[0] for a in aps)
        cols = max(int(np.prod(a.shape[1:])) for a in aps)
        dd = nc.dram_tensor("dump_" + tag + nm, [rows, cols], aps[0].dtype,
                            kind="ExternalOutput").ap()
        r0 = 0
        for a in aps:
            nr = a.shape[0]
            ncol = int(np.prod(a.shape[1:]))
            rec.emit("sync", lambda e, a=a, r0=r0, nr=nr, ncol=ncol, dd=dd: e.dma_start(
                out=dd[r0:r0 + nr, 0:ncol],
                in_=(a if a.ndim == 2 else a.rearrange(
                    "p a b -> p (a b)"))),
                inc=("sld", 16))
            r0 += nr


def emit_store(rec, g, dram, mark_done=True):
    for f in range(NF):
        rec.emit("sync", lambda e, f=f: e.dma_start(
            out=dram[f * 128:(f + 1) * 128, :], in_=g["resid"][f][:]),
            waits=(None if f == 0 else []), inc=("sld", 16))
    if mark_done:
        rec.emit("sync", lambda e: None, waits=[("sld", rec.count("sld"))])


def build_nc():
    nc = bass.Bass("TRN2", target_bir_lowering=False, debug=False,
                   detect_race_conditions=False)
    x_d = nc.dram_tensor("x", [D, TOK], F32, kind="ExternalInput").ap()
    y_d = nc.dram_tensor("y", [D, TOK], F32, kind="ExternalOutput").ap()
    w = {}
    for nm, shape, dt in (
        ("wq_a", [D, D], F32), ("wk_a", [D, D], F32), ("wv_a", [D, D], F32),
        ("wo_a", [D, D], F32),
        ("wq_f", [D, 1024], F32), ("wk_f", [D, 1024], F32), ("wv_f", [D, D], F32),
        ("wg_f", [D, D], F32), ("wout_f", [D, D], F32),
        ("conv_w", [D, 2], F32), ("router_w", [2, D], F32),
        ("lq1", [1, 64], F32), ("lk1", [1, 64], F32),
        ("lq2", [1, 64], F32), ("lk2", [1, 64], F32),
        ("gnw", [128, 16], F32), ("gnb", [128, 16], F32), ("ident", [128, 128], BF16),
    ):
        w[nm] = nc.dram_tensor(nm, shape, dt, kind="ExternalInput").ap()
    dbg = {ph: nc.dram_tensor("dbg_" + ph, [D, TOK], F32, kind="ExternalOutput").ap()
           for ph in _DEBUG}

    g = {"nc": nc, "w": w, "x_d": x_d}
    if "attn" in _PHASES:
        g.update({
            "k_sh": nc.dram_tensor("k_sh", [D, TOK], BF16),
            "v_sh": nc.dram_tensor("v_sh", [TOK, D], BF16),
            "k_all": nc.dram_tensor("k_all", [NCORES * D, TOK], BF16,
                                    addr_space="Shared"),
            "v_all": nc.dram_tensor("v_all", [NCORES * TOK, D], BF16,
                                    addr_space="Shared"),
            "mx_sh": nc.dram_tensor("mx_sh", [1, 2], F32),
            "mx_all": nc.dram_tensor("mx_all", [1, 2], F32, addr_space="Shared")})

    with contextlib.ExitStack() as stack:
        ec = stack.enter_context
        g["resid"] = [ec(nc.sbuf_tensor("res%d" % f, [128, TOK], F32)) for f in range(NF)]
        for nm, shape, dt in (
            ("onesf32", [128, 1], F32), ("ones1x128", [1, 128], F32),
            ("onesb", [128, 1], BF16), ("epsrow", [1, 1], F32),
            ("epscol", [128, 1], F32), ("identb", [128, 128], BF16),
            ("conv_s", [128, 16, 2], F32), ("gnw_s", [128, 16], F32),
            ("gnb_s", [128, 16], F32),
            ("cosb", [128, 64], F32), ("sinb", [128, 64], F32),
            ("lamv", [1, 1], F32),
            ("qmax2", [1, 1], F32), ("kmax2", [1, 1], F32), ("rmax", [1, 1], F32),
            ("mxrow", [1, 2], F32), ("negm", [1, 1], F32), ("negMcol", [128, 1], F32),
            ("meanB", [128, TOK], F32), ("rstdB", [128, TOK], F32),
        ):
            g[nm] = ec(nc.sbuf_tensor(nm, shape, dt))
        g["lnrows"] = [ec(nc.sbuf_tensor("lnr%d" % i, [1, 512], F32)) for i in range(4)]
        g["lnsq"] = [ec(nc.sbuf_tensor("lnsq%d" % i, [128, TOK], F32)) for i in range(2)]

        rec = Rec(nc, stack)
        with nc.Block() as block:
            with contextlib.ExitStack() as s0:
                for nm, shape in (("iotar", [1, 32]), ("trig", [1, 128]),
                                  ("pihalf", [1, 1]), ("lrows", [1, 256]),
                                  ("lscr", [1, 128]), ("lsum", [1, 2])):
                    g[nm] = s0.enter_context(nc.sbuf_tensor(nm, shape, F32))
                g["pbc"] = s0.enter_context(nc.psum_tensor("pbc0", [128, 512], F32))
                emit_setup(rec, g)
                rec.replay(block)

            for phase in _PHASES:
                if phase.startswith("ffn"):
                    with contextlib.ExitStack() as fs:
                        fe = fs.enter_context
                        pn = phase
                        g["yT"] = [fe(nc.sbuf_tensor("%s_yT%d" % (pn, f), [128, TOK], BF16))
                                   for f in range(NF)]
                        g["zT"] = [fe(nc.sbuf_tensor("%s_zT%d" % (pn, f), [128, TOK], BF16))
                                   for f in range(NF)]
                        g["wbuf"] = [fe(nc.sbuf_tensor("%s_wb%d" % (pn, i),
                                                       [128, NF, 256], BF16))
                                     for i in range(2)]
                        g["qb"] = [fe(nc.sbuf_tensor("%s_qb%d" % (pn, t), [128, 256], BF16))
                                   for t in range(NTT)]
                        g["kb"] = [fe(nc.sbuf_tensor("%s_kb%d" % (pn, t), [128, 256], BF16))
                                   for t in range(NTT)]
                        g["vb"] = [fe(nc.sbuf_tensor("%s_vb%d" % (pn, t), [128, 512], BF16))
                                   for t in range(NTT)]
                        g["ob"] = [g["lnsq"][0][:, 0:512], g["lnsq"][0][:, 512:1024]]
                        g["fscr"] = [g["lnsq"][1][:, 0:512], g["lnsq"][1][:, 512:1024]]
                        zstT = fe(nc.sbuf_tensor("%s_zst" % pn, [128, NTT, 512], BF16))
                        g["zst"] = [zstT[:, t, :] for t in range(NTT)]
                        _zf = zstT[:].rearrange("p a b -> p (a b)").bitcast(F32)
                        g["cscr"] = [_zf[:, 0:512], _zf[:, 512:1024]]
                        g["rwrow"] = fe(nc.sbuf_tensor("%s_rwr" % pn, [1, 2, 512], F32))
                        g["rwb"] = fe(nc.sbuf_tensor("%s_rwb" % pn, [128, 2, 512], F32))
                        names = ("lg0 lg1 da s0 s1 oh0 oh1 ms0 ms1 n0 n1 f0 f1 "
                                 "w00 w10 w11 t1s t2s d00 d10 d11 qk00 qk10 qk11 "
                                 "mu vv rs").split()
                        pair_local = {"w00", "w10", "w11", "d00", "d10", "d11",
                                      "qk00", "qk10", "qk11"}
                        g["fsc"] = {nm: [fe(nc.sbuf_tensor("%s_sc_%s%d" % (pn, nm, t),
                                                           [128, 4], F32))
                                         for t in range(4 if nm in pair_local else NTT)]
                                    for nm in names}
                        g["pproj"] = [fs.enter_context(
                            nc.psum_tensor("%s_ppj%d" % (pn, i), [128, 512], F32))
                            for i in range(4)]
                        g["ptr"] = fs.enter_context(
                            nc.psum_tensor("%s_ptr" % pn, [128, 512], BF16))
                        g["prs"] = fs.enter_context(
                            nc.psum_tensor("%s_prs" % pn, [1, 512], F32))
                        g["prq"] = fs.enter_context(
                            nc.psum_tensor("%s_prq" % pn, [1, 512], F32))
                        g["pbc"] = fs.enter_context(
                            nc.psum_tensor("%s_pbc" % pn, [128, 512], F32))
                        emit_ffn(rec, g)
                        emit_dumps(rec, g, nc, pn)
                        rec.replay(block)
                elif phase == "attn":
                    with contextlib.ExitStack() as atop:
                        ae = atop.enter_context
                        g["qT"] = [ae(nc.sbuf_tensor("qT%d" % h, [128, TOK], BF16))
                                   for h in range(H)]
                        OnTT = ae(nc.sbuf_tensor("OnTT", [128, H, TOK], BF16))
                        g["OnT"] = [OnTT[:, h, :] for h in range(H)]
                        _qw = OnTT[:].rearrange("p a b -> p (a b)").bitcast(F32)
                        g["qwst"] = [
                            _qw[:, 0:2048].rearrange("p (f c) -> p f c", c=128),
                            _qw[:, 2048:4096].rearrange("p (f c) -> p f c", c=128)]
                        with contextlib.ExitStack() as aps:
                            pe = aps.enter_context
                            g["xT"] = [pe(nc.sbuf_tensor("xT%d" % f, [128, TOK], BF16))
                                       for f in range(NF)]
                            g["kst"] = [pe(nc.sbuf_tensor("kst%d" % i, [128, TOK], BF16))
                                        for i in range(2)]
                            g["vst"] = [pe(nc.sbuf_tensor("vst%d" % i, [128, 512], BF16))
                                        for i in range(2)]
                            g["sqb"] = [g["lnsq"][1][:].bitcast(BF16)[:, 0:TOK],
                                        g["lnsq"][0][:].bitcast(BF16)[:, 0:TOK]]
                            g["wbuf"] = [pe(nc.sbuf_tensor("awb%d" % i,
                                                           [128, NF, 256], BF16))
                                         for i in range(2)]

                            g["pproj"] = [aps.enter_context(
                                nc.psum_tensor("apj%d" % i, [128, 512], F32))
                                for i in range(4)]
                            g["prs"] = aps.enter_context(
                                nc.psum_tensor("aprs", [1, 512], F32))
                            g["prq"] = aps.enter_context(
                                nc.psum_tensor("aprq", [1, 512], F32))
                            g["pbc"] = aps.enter_context(
                                nc.psum_tensor("apbc", [128, 512], F32))
                            emit_attn_proj(rec, g)
                            rec.replay(block)
                        with contextlib.ExitStack() as acs:
                            ce = acs.enter_context
                            g["kbh"] = [ce(nc.sbuf_tensor("kbh%d" % i, [128, 4096], BF16))
                                        for i in range(2)]
                            g["vbh"] = [ce(nc.sbuf_tensor("vbh%d" % i, [128, 4096], BF16))
                                        for i in range(2)]
                            g["ptile"] = [ce(nc.sbuf_tensor("ptl%d" % i, [128, 1024], BF16))
                                          for i in range(2)]
                            g["oc1"] = ce(nc.sbuf_tensor("oc1", [128, 512], F32))
                            g["oc2"] = ce(nc.sbuf_tensor("oc2", [128, 512], F32))
                            g["BaS"] = ce(nc.sbuf_tensor("BaS", [128, 512], F32))
                            g["BbS"] = ce(nc.sbuf_tensor("BbS", [128, 512], F32))
                            g["z1r"] = ce(nc.sbuf_tensor("z1r", [1, 512], F32))
                            g["z2r"] = ce(nc.sbuf_tensor("z2r", [1, 512], F32))
                            g["ar"] = ce(nc.sbuf_tensor("ar", [1, 512], F32))
                            g["br"] = ce(nc.sbuf_tensor("br", [1, 512], F32))
                            g["sp"] = [acs.enter_context(
                                nc.psum_tensor("sp%d" % i, [128, 1024], F32))
                                for i in range(2)]
                            g["pO1"] = acs.enter_context(nc.psum_tensor("pO1", [128, 512], F32))
                            g["pO2"] = acs.enter_context(nc.psum_tensor("pO2", [128, 512], F32))
                            g["zb"] = acs.enter_context(nc.psum_tensor("zbk", [128, 512], F32))
                            g["pf"] = acs.enter_context(nc.psum_tensor("pfk", [128, 512], F32))
                            emit_attn_core(rec, g)
                            rec.replay(block)
                        with contextlib.ExitStack() as aws:
                            g["wbuf"] = [aws.enter_context(
                                nc.sbuf_tensor("owb%d" % i, [128, NF, 512], BF16))
                                for i in range(2)]
                            g["pproj"] = [aws.enter_context(
                                nc.psum_tensor("opj%d" % i, [128, 512], F32))
                                for i in range(4)]
                            emit_out_proj(rec, g, w["wo_a"], g["OnT"], 1.0)
                            emit_dumps(rec, g, nc, "at")
                            rec.replay(block)
                if phase in dbg:
                    emit_store(rec, g, dbg[phase], mark_done=False)
                    rec.replay(block)

            emit_store(rec, g, y_d)
            rec.replay(block)
    return nc


def kernel(**inputs):
    f32 = np.float32
    x = np.asarray(inputs["L0_attn_x"], dtype=f32)
    ident = np.eye(128, dtype=ml_dtypes.bfloat16)
    rw = np.asarray(inputs["L0_ffn_router_w"], dtype=f32)
    rw2 = np.ascontiguousarray(rw.transpose(2, 0, 1).reshape(2, D))
    base = {
        "wq_a": np.asarray(inputs["L0_attn_Wq"], dtype=f32),
        "wk_a": np.asarray(inputs["L0_attn_Wk"], dtype=f32),
        "wv_a": np.asarray(inputs["L0_attn_Wv"], dtype=f32),
        "wo_a": np.asarray(inputs["L0_attn_Wo"], dtype=f32),
        "wq_f": np.asarray(inputs["L0_ffn_Wq"], dtype=f32),
        "wk_f": np.asarray(inputs["L0_ffn_Wk"], dtype=f32),
        "wv_f": np.asarray(inputs["L0_ffn_Wv"], dtype=f32),
        "wg_f": np.asarray(inputs["L0_ffn_Wg"], dtype=f32),
        "wout_f": np.asarray(inputs["L0_ffn_Wout"], dtype=f32),
        "conv_w": np.asarray(inputs["L0_ffn_conv_w"], dtype=f32),
        "router_w": rw2,
        "lq1": np.asarray(inputs["L0_attn_lambda_q1"], dtype=f32).reshape(1, 64),
        "lk1": np.asarray(inputs["L0_attn_lambda_k1"], dtype=f32).reshape(1, 64),
        "lq2": np.asarray(inputs["L0_attn_lambda_q2"], dtype=f32).reshape(1, 64),
        "lk2": np.asarray(inputs["L0_attn_lambda_k2"], dtype=f32).reshape(1, 64),
        "gnw": np.ascontiguousarray(
            np.asarray(inputs["L0_attn_gn_weight"], dtype=f32).reshape(16, 128).T),
        "gnb": np.ascontiguousarray(
            np.asarray(inputs["L0_attn_gn_bias"], dtype=f32).reshape(16, 128).T),
        "ident": ident,
    }
    in_maps = []
    for c in range(NCORES):
        xc = np.concatenate([x[0, c * POS:(c + 1) * POS],
                             x[1, c * POS:(c + 1) * POS]], axis=0)
        m = dict(base)
        m["x"] = np.ascontiguousarray(xc.T)
        in_maps.append(m)

    nc = build_nc()
    res = run_bass_kernel_spmd(nc, in_maps, core_ids=list(range(NCORES)))
    out = np.empty((2, S, D), np.float32)
    for c, r in enumerate(res.results):
        y = np.asarray(r["y"]).T
        out[0, c * POS:(c + 1) * POS] = y[:POS]
        out[1, c * POS:(c + 1) * POS] = y[POS:]
    return out


# version 08
# speedup vs baseline: 1.0514x; 1.0514x over previous
import contextlib

import numpy as np
import ml_dtypes

import concourse.bass as bass
import concourse.mybir as mybir
from concourse.bass_utils import run_bass_kernel_spmd

F32 = mybir.dt.float32
BF16 = mybir.dt.bfloat16
AF = mybir.ActivationFunctionType
ALU = mybir.AluOpType
AXX = mybir.AxisListType.X

NCORES = 8
D = 2048
NF = 16
TOK = 1024
POS = 512
NTT = 8
H = 16
NKT = 32
NHG = 4
S = 4096
LN_EPS = 1e-5
ATTN_SCALE = 0.125
FFN_SCALE = 0.125
LAMBDA_INIT = 0.8

_PHASES = ("ffn1", "attn", "ffn2")
_DEBUG = ()
_FFN_LIMIT = None
_FFN_SKIP_ROUTER = False
_FFN_SKIP_VEVAC = False
_DUMPS = ()
_ATTN_SERIAL = False
_Q_SERIAL = False

_OWNER = {"mm": "tensor", "smm": "tensor", "zav": "tensor",
          "act": "scalar", "sxp": "scalar",
          "dve": "vector",
          "wld": "gpsimd", "cc": "gpsimd",
          "sld": "sync"}


class Rec:
    ENGINES = ("sync", "gpsimd", "tensor", "scalar", "vector")

    def __init__(self, nc, stack):
        self.nc = nc
        self.stack = stack
        self.ops = {e: [] for e in self.ENGINES}
        self.sems = {}
        self.cnt = {}
        self.issued = {e: {} for e in self.ENGINES}

    def sem(self, name):
        if name not in self.sems:
            self.sems[name] = self.stack.enter_context(self.nc.semaphore(name))
            self.cnt.setdefault(name, 0)
        return self.sems[name]

    def count(self, name):
        return self.cnt.get(name, 0)

    def emit(self, engine, fn, waits=None, inc=None):
        if waits is None:
            waits = [(ch, self.cnt[ch]) for ch in self.cnt
                     if _OWNER[ch] != engine and self.cnt[ch] > 0]
        else:
            norm = []
            for wspec in waits:
                if isinstance(wspec, tuple):
                    norm.append(wspec)
                else:
                    norm.append((wspec, self.cnt.get(wspec, 0)))
            waits = [(n, v) for (n, v) in norm if v > 0]
        if engine == "vector":
            waits = list(waits) + [("dve", self.cnt.get("dve", 0))]
        wl = []
        iss = self.issued[engine]
        for nm, val in waits:
            if iss.get(nm, 0) >= val:
                continue
            iss[nm] = val
            wl.append((self.sem(nm), val))
        if inc is not None:
            self.sem(inc[0])
            self.cnt[inc[0]] += inc[1]
            inc = (self.sems[inc[0]], inc[1])

        def op(eng, fn=fn, wl=wl, inc=inc):
            for s, v in wl:
                eng.wait_ge(s, v)
            ins = fn(eng)
            if inc is not None:
                assert ins is not None
                ins.then_inc(inc[0], inc[1])
        self.ops[engine].append(op)

    def replay(self, block):
        for e in self.ENGINES:
            ops = self.ops[e]
            self.ops[e] = []
            if not ops:
                continue

            def body(eng, ops=ops):
                for op in ops:
                    op(eng)
            getattr(block, e)(body)


class WStream:
    def __init__(self, rec, wbufs):
        self.rec, self.wbufs = rec, wbufs
        self.mm_after_use = [rec.count("mm")] * len(wbufs)
        self.j = 0

    def load(self, fn):
        s = self.j % len(self.wbufs)
        self.j += 1
        wb = self.wbufs[s]
        self.rec.emit("gpsimd", lambda e, wb=wb, fn=fn: fn(e, wb),
                      waits=[("mm", self.mm_after_use[s])], inc=("wld", 16))
        return s, wb, self.rec.count("wld")

    def used(self, s):
        self.mm_after_use[s] = self.rec.count("mm")


class ProjPipe:
    def __init__(self, rec, banks):
        self.rec, self.banks = rec, banks
        self.act_e = [rec.count("act")] * len(banks)
        self.dve_e = [rec.count("dve")] * len(banks)
        self.i = 0

    def floor(self):
        a, d = self.rec.count("act"), self.rec.count("dve")
        self.act_e = [max(x, a) for x in self.act_e]
        self.dve_e = [max(x, d) for x in self.dve_e]

    def group(self, mk_mm, nchunks, wld=0, extra=()):
        b = self.i % len(self.banks)
        self.i += 1
        pp = self.banks[b]
        for f in range(nchunks):
            w = ([("wld", wld), ("act", self.act_e[b]), ("dve", self.dve_e[b])]
                 + list(extra)) if f == 0 else []
            self.rec.emit("tensor", mk_mm(f, pp), waits=w,
                          inc=(("mm", 1) if f == nchunks - 1 else None))
        return b, pp

    def done(self, b):
        self.act_e[b] = self.rec.count("act")
        self.dve_e[b] = self.rec.count("dve")


def emit_ln(rec, g, out_tiles=None):
    resid, onesf32, ones1x128 = g["resid"], g["onesf32"], g["ones1x128"]
    sq = g["lnsq"]
    prs, prq, pbc = g["prs"], g["prq"], g["pbc"]
    mrow, vrow, rrow, srow = g["lnrows"]
    meanB, rstdB = g["meanB"], g["rstdB"]
    epsrow = g["epsrow"]

    for half in range(2):
        c0 = half * 512
        mmq0 = rec.count("mm")
        for f in range(NF):
            rec.emit("scalar", lambda e, f=f, c0=c0: e.activation(
                sq[f % 2][:, c0:c0 + 512], resid[f][:, c0:c0 + 512], AF.Square),
                waits=(None if f < 2 else [("mm", mmq0 + f - 1)]),
                inc=("act", 1))
            rec.emit("tensor", lambda e, f=f, c0=c0: e.matmul(
                prs[:], onesf32[:], resid[f][:, c0:c0 + 512],
                start=(f == 0), stop=(f == NF - 1)),
                waits=(None if f == 0 else []),
                inc=None)
            rec.emit("tensor", lambda e, f=f, c0=c0: e.matmul(
                prq[:], onesf32[:], sq[f % 2][:, c0:c0 + 512],
                start=(f == 0), stop=(f == NF - 1)),
                waits=[("act", rec.count("act"))],
                inc=("mm", 1))
        rec.emit("vector", lambda e: e.tensor_scalar(
            mrow[:], prs[:], 1.0 / D, None, ALU.mult),
            waits=[("mm", rec.count("mm"))], inc=("dve", 1))
        rec.emit("vector", lambda e: e.tensor_scalar(
            vrow[:], prq[:], 1.0 / D, None, ALU.mult),
            waits=[], inc=("dve", 1))
        rec.emit("vector", lambda e: e.tensor_mul(
            srow[:], mrow[:], mrow[:]),
            waits=[], inc=("dve", 1))
        rec.emit("vector", lambda e: e.tensor_sub(
            vrow[:], vrow[:], srow[:]),
            waits=[], inc=("dve", 1))
        rec.emit("scalar", lambda e: e.activation(
            srow[:], vrow[:], AF.Sqrt, bias=epsrow[:]),
            waits=[("dve", rec.count("dve"))], inc=("act", 1))
        rec.emit("vector", lambda e: e.reciprocal(
            rrow[:], srow[:]),
            waits=[("act", rec.count("act"))], inc=("dve", 1))
        rec.emit("tensor", lambda e: e.matmul(
            pbc[:], ones1x128[:], mrow[:], start=True, stop=True),
            waits=[("dve", rec.count("dve"))], inc=("mm", 1))
        rec.emit("scalar", lambda e, c0=c0: e.copy(
            meanB[:, c0:c0 + 512], pbc[:]),
            waits=[("mm", rec.count("mm"))], inc=("act", 1))
        rec.emit("tensor", lambda e: e.matmul(
            pbc[:], ones1x128[:], rrow[:], start=True, stop=True),
            waits=[("act", rec.count("act"))], inc=("mm", 1))
        rec.emit("scalar", lambda e, c0=c0: e.copy(
            rstdB[:, c0:c0 + 512], pbc[:]),
            waits=[("mm", rec.count("mm"))], inc=("act", 1))
        g["pbc_act"] = rec.count("act")

    if out_tiles is not None:
        for f in range(NF):
            rec.emit("vector", lambda e, f=f: e.tensor_sub(
                sq[f % 2][:], resid[f][:], meanB[:]),
                waits=(None if f < 2 else []), inc=("dve", 1))
            rec.emit("vector", lambda e, f=f: e.tensor_mul(
                out_tiles[f][:], sq[f % 2][:], rstdB[:]), waits=[], inc=("dve", 1))


def emit_out_proj(rec, g, w_ap, src_tiles, scale):
    wbuf = g["wbuf"]
    pipe = ProjPipe(rec, g["pproj"])
    ws = WStream(rec, wbuf)
    wcols = wbuf[0].ap().shape[2]

    def loader(cbase):
        def fn(e, wb):
            return e.dma_start(
                out=wb[:, :, 0:wcols],
                in_=w_ap[:, cbase:cbase + wcols].rearrange("(f p) c -> p f c", p=128))
        return fn

    def run_unit(cbase, wb, wld):
        for sub in range(wcols // 128):
            of = (cbase + sub * 128) // 128
            for half in range(2):
                def mk(f, pp, sub=sub, half=half, wb=wb):
                    return lambda e: e.matmul(
                        pp[:], wb[:, f, sub * 128:(sub + 1) * 128],
                        src_tiles[f][:, half * 512:(half + 1) * 512],
                        start=(f == 0), stop=(f == NF - 1))
                b, pp = pipe.group(mk, NF, wld=wld)
                rec.emit("vector", lambda e, of=of, half=half, pp=pp: e.scalar_tensor_tensor(
                    g["resid"][of][:, half * 512:(half + 1) * 512], pp[:], scale,
                    g["resid"][of][:, half * 512:(half + 1) * 512], ALU.mult, ALU.add),
                    waits=[("mm", rec.count("mm"))], inc=("dve", 1))
                pipe.done(b)

    prev = None
    for cbase in range(0, D, wcols):
        s, wb, wld = ws.load(loader(cbase))
        if prev is not None:
            run_unit(*prev[1:])
            ws.used(prev[0])
        prev = (s, cbase, wb, wld)
    run_unit(*prev[1:])
    ws.used(prev[0])


def emit_ffn(rec, g):
    resid = g["resid"]
    yT, wbuf = g["yT"], g["wbuf"]
    qb, kb, vb, ob = g["qb"], g["kb"], g["vb"], g["ob"]
    scr, cscr = g["fscr"], g["cscr"]
    scr2 = cscr
    zT = g["zT"]
    sc = g["fsc"]
    pproj, ptr = g["pproj"], g["ptr"]
    meanB, rstdB = g["meanB"], g["rstdB"]
    conv_s, cosb, sinb = g["conv_s"], g["cosb"], g["sinb"]
    rwb, rwrow = g["rwb"], g["rwrow"]
    epscol = g["epscol"]
    wf = g["w"]
    lnsq = g["lnsq"]

    pipe = ProjPipe(rec, g["pproj"])
    ws = WStream(rec, wbuf)
    g["pending_tr"] = None
    g["pending_tail"] = None

    emit_ln(rec, g, None)
    if _FFN_LIMIT == "ln":
        return

    for f in range(NF):
        rec.emit("vector", lambda e, f=f: e.tensor_sub(
            lnsq[f % 2][:], resid[f][:], meanB[:]),
            waits=(None if f < 2 else [("act", rec.count("act"))]), inc=("dve", 1))
        rec.emit("vector", lambda e, f=f: e.tensor_mul(
            lnsq[f % 2][:], lnsq[f % 2][:], rstdB[:]), waits=[], inc=("dve", 1))
        rec.emit("scalar", lambda e, f=f: e.activation(
            yT[f][:, 0:512], lnsq[f % 2][:, 0:512], AF.Silu,
            scale=conv_s[:, f, 1:2]),
            waits=[("dve", rec.count("dve"))], inc=("act", 1))
        rec.emit("vector", lambda e, f=f: e.tensor_scalar(
            cscr[f % 2][:], lnsq[f % 2][:, 0:512], conv_s[:, f, 0:1],
            None, ALU.mult), waits=[], inc=("dve", 1))
        rec.emit("vector", lambda e, f=f: e.scalar_tensor_tensor(
            cscr[f % 2][:], lnsq[f % 2][:, 512:1024], conv_s[:, f, 1:2],
            cscr[f % 2][:], ALU.mult, ALU.add), waits=[], inc=("dve", 1))
        rec.emit("scalar", lambda e, f=f: e.activation(
            yT[f][:, 512:1024], cscr[f % 2][:], AF.Silu),
            waits=[("dve", rec.count("dve"))], inc=("act", 1))

    if _FFN_LIMIT == "conv":
        return
    pipe.floor()

    def V(fn, waits=()):
        rec.emit("vector", fn, waits=list(waits), inc=("dve", 1))

    for hg in range(NHG):
        qc0, vc0 = hg * 256, hg * 512

        rec.emit("gpsimd", lambda e, vc0=vc0: e.dma_start(
            out=rwrow[:], in_=wf["router_w"][:, vc0:vc0 + 512].unsqueeze(0)),
            waits=[("dve", rec.count("dve"))], inc=("wld", 16))
        for r in range(2):
            rec.emit("tensor", lambda e, r=r: e.matmul(
                g["pbc"][:], g["ones1x128"][:], rwrow[:, r, :],
                start=True, stop=True),
                waits=[("act", g.get("pbc_act", 0)), ("wld", rec.count("wld"))],
                inc=("mm", 1))
            rec.emit("scalar", lambda e, r=r: e.copy(
                rwb[:, r, 0:512], g["pbc"][:]),
                waits=[("mm", rec.count("mm"))], inc=("act", 1))
            g["pbc_act"] = rec.count("act")

        def unit_loader(wname, c0, n):
            def fn(e, wb, wname=wname, c0=c0, n=n):
                return e.dma_start(
                    out=wb[:, :, 0:n],
                    in_=wf[wname][:, c0:c0 + n].rearrange("(f p) c -> p f c", p=128))
            return fn

        sq_, wbq, wldq = ws.load(unit_loader("wq_f", qc0, 256))
        sk_, wbk, wldk = ws.load(unit_loader("wk_f", qc0, 256))
        for which in range(2):
            wb = wbq if which == 0 else wbk
            wld = wldq if which == 0 else wldk
            for tt in range(NTT):
                def mk(f, pp, tt=tt, wb=wb):
                    return lambda e: e.matmul(
                        pp[:, 0:256], yT[f][:, tt * 128:(tt + 1) * 128],
                        wb[:, f, 0:256], start=(f == 0), stop=(f == NF - 1))
                b, pp = pipe.group(mk, NF, wld=wld)
                if which == 0:
                    rec.emit("scalar", lambda e, tt=tt, pp=pp: e.activation(
                        qb[tt][:], pp[:, 0:256], AF.Silu),
                        waits=[("mm", rec.count("mm"))], inc=("act", 1))
                else:
                    rec.emit("scalar", lambda e, tt=tt, pp=pp: e.copy(
                        kb[tt][:], pp[:, 0:256]),
                        waits=[("mm", rec.count("mm"))], inc=("act", 1))
                pipe.done(b)
            ws.used(sq_ if which == 0 else sk_)

        if _FFN_LIMIT == ("qk", hg):
            return
        if g.get("pending_tail") is not None:
            g["pending_tail"]()
            g["pending_tail"] = None
        act_qk = rec.count("act")
        mm_tr = rec.count("mm")
        for dst in (qb, kb):
            for tt in range(4, 8):
                A, Bs = scr[tt % 2], ob[tt % 2]
                V(lambda e, dst=dst, tt=tt, A=A: e.tensor_tensor(
                    A[:, 0:256].rearrange("p (h j) -> p h j", j=64),
                    dst[tt][:].rearrange("p (h j) -> p h j", j=64),
                    cosb[:].unsqueeze(1).broadcast_to([128, 4, 64]), ALU.mult),
                    waits=[("act", act_qk), ("mm", mm_tr)])
                V(lambda e, dst=dst, tt=tt, Bs=Bs: e.tensor_tensor(
                    Bs[:, 0:256].rearrange("p (h j) -> p h j", j=64),
                    dst[tt][:].rearrange("p (h j) -> p h j", j=64),
                    sinb[:].unsqueeze(1).broadcast_to([128, 4, 64]), ALU.mult))

                def rot(e, dst=dst, tt=tt, A=A, Bs=Bs):
                    d3 = dst[tt][:].rearrange("p (h u j) -> p h u j", u=2, j=32)
                    a3 = A[:, 0:256].rearrange("p (h u j) -> p h u j", u=2, j=32)
                    b3 = Bs[:, 0:256].rearrange("p (h u j) -> p h u j", u=2, j=32)
                    e.tensor_sub(d3[:, :, 0, :], a3[:, :, 0, :], b3[:, :, 1, :])
                    return e.tensor_add(d3[:, :, 1, :], a3[:, :, 1, :], b3[:, :, 0, :])
                V(rot)

        if _FFN_LIMIT == ("rot", hg):
            return
        sv0, wbv0, _ = ws.load(unit_loader("wv_f", vc0, 256))
        sv1, wbv1, wldv = ws.load(unit_loader("wv_f", vc0 + 256, 256))
        if _FFN_LIMIT == ("vload", hg):
            return
        for tt in range(NTT):
            def mkv(f, pp, tt=tt, wbv0=wbv0, wbv1=wbv1):
                uu, fc = divmod(f, NF)
                wb = wbv0 if uu == 0 else wbv1
                return lambda e: e.matmul(
                    pp[:, uu * 256:(uu + 1) * 256],
                    yT[fc][:, tt * 128:(tt + 1) * 128], wb[:, fc, 0:256],
                    start=(fc == 0), stop=(fc == NF - 1))
            b, pp = pipe.group(mkv, 2 * NF, wld=wldv)
            if not _FFN_SKIP_VEVAC:
                rec.emit("scalar", lambda e, tt=tt, pp=pp: e.copy(
                    vb[tt][:], pp[:]),
                    waits=[("mm", rec.count("mm")), ("dve", rec.count("dve"))],
                    inc=("act", 1))
            if not _FFN_SKIP_ROUTER:
                for r in range(2):
                    V(lambda e, tt=tt, pp=pp, r=r: e.tensor_tensor(
                        ob[tt % 2][:], pp[:], rwb[:, r, 0:512], ALU.mult),
                        waits=[("mm", rec.count("mm")), ("act", rec.count("act"))])
                    V(lambda e, tt=tt, r=r: e.reduce_sum(
                        sc["lg%d" % r][:, tt, :],
                        ob[tt % 2][:].rearrange("p (h w) -> p h w", w=128), axis=AXX))
            pipe.done(b)
        ws.used(sv0)
        ws.used(sv1)

        if _FFN_LIMIT == ("v", hg):
            return
        for j in range(4):
            for nm, qi, ki in (("d00", j, j), ("d10", j + 4, j), ("d11", j + 4, j + 4)):
                V(lambda e, qi=qi, ki=ki, j=j: e.tensor_tensor(
                    scr[j % 2][:, 0:256], qb[qi][:], kb[ki][:], ALU.mult))
                V(lambda e, nm=nm, j=j: e.reduce_sum(
                    sc[nm][:, j, :],
                    scr[j % 2][:, 0:256].rearrange("p (h w) -> p h w", w=64), axis=AXX))

        if _FFN_LIMIT == ("dots", hg):
            return
        V(lambda e: e.tensor_sub(sc["da"][:], sc["lg0"][:], sc["lg1"][:]))
        rec.emit("scalar", lambda e: e.activation(
            sc["s0"][:], sc["da"][:], AF.Sigmoid),
            waits=[("dve", rec.count("dve"))], inc=("act", 1))
        rec.emit("scalar", lambda e: e.activation(
            sc["s1"][:], sc["da"][:], AF.Sigmoid, scale=-1.0),
            waits=[], inc=("act", 1))
        V(lambda e: e.tensor_single_scalar(sc["oh0"][:], sc["da"][:], 0.0, ALU.is_ge))
        V(lambda e: e.tensor_scalar(sc["oh1"][:], sc["oh0"][:], -1.0, 1.0,
                                    ALU.mult, ALU.add))
        V(lambda e: e.tensor_mul(sc["ms0"][:], sc["s0"][:], sc["oh0"][:]),
          waits=[("act", rec.count("act"))])
        V(lambda e: e.tensor_mul(sc["ms1"][:], sc["s1"][:], sc["oh1"][:]))
        V(lambda e: e.tensor_add(sc["w00"][:], sc["ms0"][:, 0:4, :], sc["ms1"][:, 0:4, :]))
        for r in ("0", "1"):
            V(lambda e, r=r: e.tensor_add(
                sc["n" + r][:], sc["oh" + r][:, 0:4, :], sc["oh" + r][:, 4:8, :]))
            V(lambda e, r=r: e.tensor_scalar_max(sc["t1s"][:], sc["n" + r][:], 1.0))
            V(lambda e, r=r: e.reciprocal(sc["t2s"][:], sc["t1s"][:]))
            V(lambda e, r=r: e.tensor_single_scalar(
                sc["t1s"][:], sc["n" + r][:], 0.5, ALU.is_ge))
            V(lambda e, r=r: e.tensor_mul(sc["n" + r][:], sc["t1s"][:], sc["t2s"][:]))
            V(lambda e, r=r: e.tensor_mul(
                sc["f" + r][:], sc["ms" + r][:, 4:8, :], sc["n" + r][:]))
        V(lambda e: e.tensor_mul(sc["t1s"][:], sc["f0"][:], sc["oh0"][:, 0:4, :]))
        V(lambda e: e.tensor_mul(sc["t2s"][:], sc["f1"][:], sc["oh1"][:, 0:4, :]))
        V(lambda e: e.tensor_add(sc["w10"][:], sc["t1s"][:], sc["t2s"][:]))
        V(lambda e: e.tensor_add(sc["w11"][:], sc["f0"][:], sc["f1"][:]))
        for nm in ("00", "10", "11"):
            V(lambda e, nm=nm: e.scalar_tensor_tensor(
                sc["qk" + nm][:], sc["d" + nm][:], FFN_SCALE,
                sc["w" + nm][:], ALU.mult, ALU.mult))

        if _FFN_LIMIT == ("scal", hg):
            return
        if g.get("pending_tr") is not None:
            g["pending_tr"]()
            g["pending_tr"] = None
        sg0, wbg0, _ = ws.load(unit_loader("wg_f", vc0, 256))
        sg1, wbg1, wldg = ws.load(unit_loader("wg_f", vc0 + 256, 256))
        zst = g["zst"]
        for tt in range(NTT):
            def mkg(f, pp, tt=tt, wbg0=wbg0, wbg1=wbg1):
                uu, fc = divmod(f, NF)
                wb = wbg0 if uu == 0 else wbg1
                return lambda e: e.matmul(
                    pp[:, uu * 256:(uu + 1) * 256],
                    yT[fc][:, tt * 128:(tt + 1) * 128], wb[:, fc, 0:256],
                    start=(fc == 0), stop=(fc == NF - 1))
            b, pp = pipe.group(mkg, 2 * NF, wld=wldg)
            rec.emit("scalar", lambda e, tt=tt, pp=pp: e.activation(
                zst[tt][:], pp[:], AF.Silu),
                waits=[("mm", rec.count("mm")), ("dve", rec.count("dve"))],
                inc=("act", 1))
            pipe.done(b)
        ws.used(sg0)
        ws.used(sg1)

        def make_tail(hg=hg):
            def bcq(nm, j):
                return sc[nm][:, j, :].unsqueeze(2).broadcast_to([128, 4, 128])

            def bcp(nm, i):
                return sc[nm][:, i, :].unsqueeze(2).broadcast_to([128, 4, 128])

            def emit_tail():
                zc = {}
                first = True
                for j in range(4):
                    t0, t1 = j, j + 4
                    V(lambda e, j=j, t0=t0: e.tensor_tensor(
                        ob[0][:].rearrange("p (h w) -> p h w", w=128),
                        bcq("qk00", j),
                        vb[t0][:].rearrange("p (h w) -> p h w", w=128), ALU.mult),
                        waits=([("act", rec.count("act"))] if first else []))
                    first = False
                    V(lambda e, j=j, t0=t0: e.tensor_tensor(
                        scr[0][:].rearrange("p (h w) -> p h w", w=128),
                        bcq("qk10", j),
                        vb[t0][:].rearrange("p (h w) -> p h w", w=128), ALU.mult))
                    V(lambda e, j=j, t1=t1: e.tensor_tensor(
                        scr[1][:].rearrange("p (h w) -> p h w", w=128),
                        bcq("qk11", j),
                        vb[t1][:].rearrange("p (h w) -> p h w", w=128), ALU.mult))
                    V(lambda e: e.tensor_add(ob[1][:], scr[0][:], scr[1][:]))
                    for oi in range(2):
                        o3 = ob[oi][:].rearrange("p (h w) -> p h w", w=128)
                        V(lambda e, oi=oi, o3=o3: e.reduce_sum(
                            sc["muP"][:, oi, :], o3, axis=AXX))
                        V(lambda e, oi=oi: e.tensor_mul(
                            scr[0][:], ob[oi][:], ob[oi][:]))
                        V(lambda e, oi=oi: e.reduce_sum(
                            sc["vvP"][:, oi, :],
                            scr[0][:].rearrange("p (h w) -> p h w", w=128), axis=AXX))
                    V(lambda e: e.tensor_scalar(
                        sc["muP"][:], sc["muP"][:], 1.0 / 128, None, ALU.mult))
                    V(lambda e: e.tensor_scalar(
                        sc["vvP"][:], sc["vvP"][:], 1.0 / 128, None, ALU.mult))
                    V(lambda e: e.tensor_mul(sc["t3P"][:], sc["muP"][:], sc["muP"][:]))
                    V(lambda e: e.tensor_sub(sc["vvP"][:], sc["vvP"][:], sc["t3P"][:]))
                    rec.emit("scalar", lambda e: e.activation(
                        sc["rsP"][:], sc["vvP"][:], AF.Sqrt, bias=epscol[:]),
                        waits=[("dve", rec.count("dve"))], inc=("act", 1))
                    rec.emit("vector", lambda e: e.reciprocal(
                        sc["rsP"][:], sc["rsP"][:]),
                        waits=[("act", rec.count("act"))], inc=("dve", 1))
                    for oi, tt in ((0, t0), (1, t1)):
                        o3 = ob[oi][:].rearrange("p (h w) -> p h w", w=128)
                        V(lambda e, oi=oi, o3=o3: e.tensor_tensor(
                            o3, o3, bcp("muP", oi), ALU.subtract))
                        V(lambda e, oi=oi, o3=o3: e.tensor_tensor(
                            o3, o3, bcp("rsP", oi), ALU.mult))
                        V(lambda e, tt=tt, oi=oi: e.tensor_mul(
                            zst[tt][:], zst[tt][:], ob[oi][:]))
                        zc[tt] = rec.count("dve")

                def emit_tr(zc=zc, hg=hg):
                    for tt in range(NTT):
                        for c in range(4):
                            rec.emit("tensor", lambda e, tt=tt, c=c: e.transpose(
                                ptr[:, c * 128:(c + 1) * 128],
                                zst[tt][:, c * 128:(c + 1) * 128], g["identb"][:]),
                                waits=([("dve", max(zc[tt], rec.count("dve")))]
                                       if c == 0 else []),
                                inc=(("mm", 1) if c == 3 else None))

                        def evac_tr(e, tt=tt, hg=hg):
                            last = None
                            for c in range(4):
                                last = e.tensor_copy(
                                    zT[hg * 4 + c][:, tt * 128:(tt + 1) * 128],
                                    ptr[:, c * 128:(c + 1) * 128])
                            return last
                        rec.emit("vector", evac_tr,
                                 waits=[("mm", rec.count("mm"))], inc=("dve", 1))
                g["pending_tr"] = emit_tr
            return emit_tail
        g["pending_tail"] = make_tail()

    if g.get("pending_tail") is not None:
        g["pending_tail"]()
        g["pending_tail"] = None
    if g.get("pending_tr") is not None:
        g["pending_tr"]()
        g["pending_tr"] = None
        emit_out_proj(rec, g, wf["wout_f"], zT, 0.5)


def emit_attn_proj(rec, g):
    wf = g["w"]
    xT, wbuf = g["xT"], g["wbuf"]
    qT, kst, vst = g["qT"], g["kst"], g["vst"]
    sqb = g["sqb"]
    prow = g["prs"]
    onesb, onesf32, ones1x128 = g["onesb"], g["onesf32"], g["ones1x128"]

    emit_ln(rec, g, out_tiles=xT)
    pipe = ProjPipe(rec, g["pproj"])
    ws = WStream(rec, wbuf)

    def unit_loader(wname, c0, n):
        def fn(e, wb, wname=wname, c0=c0, n=n):
            return e.dma_start(
                out=wb[:, :, 0:n],
                in_=wf[wname][:, c0:c0 + n].rearrange("(f p) c -> p f c", p=128))
        return fn

    def emit_bound(buf, nm):
        for half in range(2):
            rec.emit("tensor", lambda e, buf=buf, half=half: e.matmul(
                prow[:], onesb[:], buf[:, half * 512:(half + 1) * 512],
                start=True, stop=True),
                waits=[("dve", rec.count("dve"))], inc=("mm", 1))
            rec.emit("vector", lambda e: e.reduce_max(
                g["rmax"][:], prow[:], axis=AXX),
                waits=[("mm", rec.count("mm"))], inc=("dve", 1))
            rec.emit("vector", lambda e, nm=nm: e.tensor_max(
                g[nm][:], g[nm][:], g["rmax"][:]), waits=[], inc=("dve", 1))

    pending_bound = None
    for h in range(H):
        s_, wb, wld = ws.load(unit_loader("wk_a", h * 128, 128))
        dst = kst[h % 2]
        for half in range(2):
            def mk(f, pp, half=half, wb=wb):
                return lambda e: e.matmul(
                    pp[:], wb[:, f, 0:128],
                    xT[f][:, half * 512:(half + 1) * 512],
                    start=(f == 0), stop=(f == NF - 1))
            b, pp = pipe.group(mk, NF, wld=wld)
            rec.emit("scalar", lambda e, dst=dst, half=half, pp=pp: e.copy(
                dst[:, half * 512:(half + 1) * 512], pp[:]),
                waits=[("mm", rec.count("mm")), ("sld", rec.count("sld")),
                       ("dve", rec.count("dve"))],
                inc=("act", 1))
            pipe.done(b)
        ws.used(s_)
        buf = sqb[h % 2]
        rec.emit("vector", lambda e, dst=dst, buf=buf: e.tensor_mul(
            buf[:], dst[:], dst[:]),
            waits=[("act", rec.count("act")), ("mm", rec.count("mm"))],
            inc=("dve", 1))
        if pending_bound is not None:
            pending_bound()
        pending_bound = (lambda buf=buf: emit_bound(buf[:], "kmax2"))
        rec.emit("sync", lambda e, h=h, dst=dst: e.dma_start(
            out=g["k_sh"].ap()[h * 128:(h + 1) * 128, :], in_=dst[:]),
            waits=[("act", rec.count("act"))], inc=("sld", 16))
    pending_bound()
    pending_bound = None

    for oc in range(8):
        s_, wb, wld = ws.load(unit_loader("wv_a", oc * 256, 256))
        for tt in range(NTT):
            def mkv(f, pp, tt=tt, wb=wb):
                return lambda e: e.matmul(
                    pp[:, 0:256], xT[f][:, tt * 128:(tt + 1) * 128],
                    wb[:, f, 0:256], start=(f == 0), stop=(f == NF - 1))
            b, pp = pipe.group(mkv, NF, wld=wld)
            rec.emit("scalar", lambda e, tt=tt, pp=pp: e.copy(
                vst[tt % 2][:, 0:256], pp[:, 0:256]),
                waits=[("mm", rec.count("mm")), ("sld", rec.count("sld"))],
                inc=("act", 1))
            rec.emit("sync", lambda e, tt=tt, oc=oc: e.dma_start(
                out=g["v_sh"].ap()[tt * 128:(tt + 1) * 128, oc * 256:(oc + 1) * 256],
                in_=vst[tt % 2][:, 0:256]),
                waits=[("act", rec.count("act"))], inc=("sld", 16))
            pipe.done(b)
        ws.used(s_)

    rec.emit("gpsimd", lambda e: e.collective_compute(
        "AllGather", ALU.bypass, replica_groups=[list(range(NCORES))],
        ins=[g["k_sh"].ap().opt()], outs=[g["k_all"].ap().opt()]),
        waits=[("sld", rec.count("sld"))], inc=("cc", 1))
    rec.emit("gpsimd", lambda e: e.collective_compute(
        "AllGather", ALU.bypass, replica_groups=[list(range(NCORES))],
        ins=[g["v_sh"].ap().opt()], outs=[g["v_all"].ap().opt()]),
        waits=[], inc=("cc", 1))
    g["cc_after_gather"] = rec.count("cc")

    rec.emit("gpsimd", lambda e: None, waits=[("cc", rec.count("cc"))])
    for h in range(H):
        s_, wb, wld = ws.load(unit_loader("wq_a", h * 128, 128))
        for half in range(2):
            def mk(f, pp, half=half, wb=wb):
                return lambda e: e.matmul(
                    pp[:], wb[:, f, 0:128],
                    xT[f][:, half * 512:(half + 1) * 512],
                    start=(f == 0), stop=(f == NF - 1))
            b, pp = pipe.group(mk, NF, wld=wld)
            rec.emit("scalar", lambda e, h=h, half=half, pp=pp: e.copy(
                qT[h][:, half * 512:(half + 1) * 512], pp[:]),
                waits=[("mm", rec.count("mm")), ("dve", rec.count("dve"))],
                inc=("act", 1))
            pipe.done(b)
        ws.used(s_)
        buf = sqb[h % 2]
        rec.emit("vector", lambda e, h=h, buf=buf: e.tensor_mul(
            buf[:], qT[h][:], qT[h][:]),
            waits=[("act", rec.count("act")), ("mm", rec.count("mm"))],
            inc=("dve", 1))
        if pending_bound is not None:
            pending_bound()
        pending_bound = (lambda buf=buf: emit_bound(buf[:], "qmax2"))
    pending_bound()

    rec.emit("vector", lambda e: e.tensor_copy(g["mxrow"][:, 0:1], g["qmax2"][:]),
             inc=("dve", 1))
    rec.emit("vector", lambda e: e.tensor_copy(g["mxrow"][:, 1:2], g["kmax2"][:]),
             waits=[], inc=("dve", 1))
    rec.emit("sync", lambda e: e.dma_start(out=g["mx_sh"].ap(), in_=g["mxrow"][:]),
             waits=[("dve", rec.count("dve"))], inc=("sld", 16))
    rec.emit("gpsimd", lambda e: e.collective_compute(
        "AllReduce", ALU.max, replica_groups=[list(range(NCORES))],
        ins=[g["mx_sh"].ap().opt()], outs=[g["mx_all"].ap().opt()]),
        waits=[("sld", rec.count("sld"))], inc=("cc", 1))
    rec.emit("sync", lambda e: e.dma_start(out=g["mxrow"][:], in_=g["mx_all"].ap()),
             waits=[("cc", rec.count("cc"))], inc=("sld", 16))
    rec.emit("scalar", lambda e: e.activation(g["mxrow"][:], g["mxrow"][:], AF.Sqrt),
             waits=[("sld", rec.count("sld"))], inc=("act", 1))
    rec.emit("vector", lambda e: e.tensor_mul(
        g["negm"][:], g["mxrow"][:, 0:1], g["mxrow"][:, 1:2]),
        waits=[("act", rec.count("act"))], inc=("dve", 1))
    rec.emit("vector", lambda e: e.tensor_scalar(
        g["negm"][:], g["negm"][:], -ATTN_SCALE, None, ALU.mult),
        waits=[], inc=("dve", 1))
    rec.emit("tensor", lambda e: e.matmul(
        g["pbc"][:, 0:1], ones1x128[:], g["negm"][:], start=True, stop=True),
        waits=[("dve", rec.count("dve"))], inc=("mm", 1))
    rec.emit("scalar", lambda e: e.copy(g["negMcol"][:], g["pbc"][:, 0:1]),
             waits=[("mm", rec.count("mm"))], inc=("act", 1))


def emit_attn_core(rec, g):
    qT, OnT = g["qT"], g["OnT"]
    kbh, vbh, ptile = g["kbh"], g["vbh"], g["ptile"]
    sp, pO1, pO2, zb, pf = g["sp"], g["pO1"], g["pO2"], g["zb"], g["pf"]
    onesb, onesf32, ones1x128 = g["onesb"], g["onesf32"], g["ones1x128"]
    oc1, oc2 = g["oc1"], g["oc2"]
    kall = g["k_all"].ap().rearrange("(r hh p) t -> p hh r t", r=NCORES, p=128)
    vall = g["v_all"].ap().rearrange("(r bb plc p) c -> p bb r plc c",
                                     r=NCORES, bb=2, plc=4, p=128)
    base_smm = rec.count("smm")
    base_sxp = rec.count("sxp")
    base_zav = rec.count("zav")
    act_pre = rec.count("act")
    sld_base = rec.count("sld")

    KV_DMAS = 1 + NCORES

    def kv_load(bh):
        b, h = bh // H, bh % H
        rec.emit("sync", lambda e, b=b, h=h, bh=bh: e.dma_start(
            out=kbh[bh % 2][:].rearrange("p (r t) -> p r t", r=NCORES),
            in_=kall[:, h, :, b * 512:(b + 1) * 512]),
            waits=([("cc", g["cc_after_gather"])] if bh < 2 else
                   [("zav", base_zav + NKT * (bh - 1))]),
            inc=("sld", 16))
        for r in range(NCORES):
            rec.emit("sync", lambda e, b=b, h=h, bh=bh, r=r: e.dma_start(
                out=vbh[bh % 2][:, r * 512:(r + 1) * 512]
                .rearrange("p (plc c) -> p plc c", c=128),
                in_=vall[:, b, r, :, h * 128:(h + 1) * 128]),
                waits=[], inc=("sld", 16))

    kv_load(0)
    kv_load(1)
    emit_finalize = emit_finalize_mk(rec, g)

    NT_ALL = 2 * H * NKT

    def s_wait(gt):
        bh, t = divmod(gt, NKT)
        w = [("sxp", base_sxp + gt - 1)] if gt >= 1 else [("act", act_pre)]
        if t == 0:
            sld_need = sld_base + 16 * KV_DMAS * min(bh + 1, 2)
            if bh >= 2:
                sld_need = sld_base + 16 * KV_DMAS * (bh + 1)
            w = w + [("sld", sld_need)]
        return w

    def emit_s(gt):
        bh, t = divmod(gt, NKT)
        kv = kbh[bh % 2]

        def s_pair(e, kv=kv, h=bh % H, b=bh // H, t=t):
            e.matmul(sp[t % 2][:, 0:512], kv[0:64, t * 128:(t + 1) * 128],
                     qT[h][0:64, b * 512:(b + 1) * 512],
                     start=True, stop=True, tile_position=(0, 0))
            return e.matmul(sp[t % 2][:, 512:1024],
                            kv[64:128, t * 128:(t + 1) * 128],
                            qT[h][64:128, b * 512:(b + 1) * 512],
                            start=True, stop=True, tile_position=(64, 0))
        rec.emit("tensor", s_pair, waits=(None if _ATTN_SERIAL else s_wait(gt)),
                 inc=("smm", 1))

    emit_s(0)
    emit_s(1)
    dve_fin_prev = rec.count("dve")
    for gt in range(NT_ALL):
        bh, t = divmod(gt, NKT)
        b, h = bh // H, bh % H
        vv = vbh[bh % 2]
        if t == 0:
            if 2 <= bh + 1 < 2 * H:
                kv_load(bh + 1)
        if gt + 2 < NT_ALL:
            emit_s(gt + 2)

        w_e = [("smm", base_smm + gt + 1)]
        if gt >= 2:
            w_e.append(("zav", base_zav + gt - 1))
        rec.emit("scalar", lambda e, t=t: e.activation(
            ptile[t % 2][:], sp[t % 2][:], AF.Exp,
            bias=g["negMcol"][:], scale=ATTN_SCALE),
            waits=(None if _ATTN_SERIAL else w_e), inc=("sxp", 1))

        w_z = [("sxp", base_sxp + gt + 1)]
        if t == 0:
            w_z.append(("dve", dve_fin_prev))

        def z_pair(e, t=t):
            e.matmul(zb[0:1, :], onesb[:], ptile[t % 2][:, 0:512],
                     start=(t == 0), stop=(t == NKT - 1), tile_position=(0, 0))
            return e.matmul(zb[32:33, :], onesb[:], ptile[t % 2][:, 512:1024],
                            start=(t == 0), stop=(t == NKT - 1),
                            tile_position=(0, 32))
        rec.emit("tensor", z_pair, waits=(None if _ATTN_SERIAL else w_z), inc=None)

        def av_pair(e, t=t, vv=vv):
            e.matmul(pO1[:], vv[:, t * 128:(t + 1) * 128], ptile[t % 2][:, 0:512],
                     start=(t == 0), stop=(t == NKT - 1))
            return e.matmul(pO2[:], vv[:, t * 128:(t + 1) * 128],
                            ptile[t % 2][:, 512:1024],
                            start=(t == 0), stop=(t == NKT - 1))
        rec.emit("tensor", av_pair, waits=[], inc=("zav", 1))

        if t == NKT - 1:
            emit_finalize(bh)
            dve_fin_prev = rec.count("dve")


def emit_finalize_mk(rec, g):
    OnT = g["OnT"]
    zb, pf, pO1, pO2 = g["zb"], g["pf"], g["pO1"], g["pO2"]
    onesf32, ones1x128 = g["onesf32"], g["ones1x128"]
    oc1, oc2 = g["oc1"], g["oc2"]

    def emit_finalize(bh):
        b, h = bh // H, bh % H
        zav_done = [("zav", rec.count("zav"))]
        rec.emit("vector", lambda e: e.tensor_copy(g["z1r"][:], zb[0:1, :]),
                 waits=zav_done, inc=("dve", 1))
        rec.emit("vector", lambda e: e.tensor_copy(g["z2r"][:], zb[32:33, :]),
                 waits=[], inc=("dve", 1))
        rec.emit("vector", lambda e: e.reciprocal(g["ar"][:], g["z1r"][:]),
                 waits=[], inc=("dve", 1))
        rec.emit("vector", lambda e: e.reciprocal(g["br"][:], g["z2r"][:]),
                 waits=[], inc=("dve", 1))
        rec.emit("vector", lambda e: e.tensor_scalar(
            g["br"][:], g["br"][:], g["lamv"][:], None, ALU.mult),
            waits=[], inc=("dve", 1))
        rec.emit("tensor", lambda e: e.matmul(
            pf[:], ones1x128[:], g["ar"][:], start=True, stop=True),
            waits=[("dve", rec.count("dve"))], inc=("mm", 1))
        rec.emit("scalar", lambda e: e.copy(g["BaS"][:], pf[:]),
                 waits=[("mm", rec.count("mm"))], inc=("act", 1))
        rec.emit("tensor", lambda e: e.matmul(
            pf[:], ones1x128[:], g["br"][:], start=True, stop=True),
            waits=[("act", rec.count("act"))], inc=("mm", 1))
        rec.emit("scalar", lambda e: e.copy(g["BbS"][:], pf[:]),
                 waits=[("mm", rec.count("mm"))], inc=("act", 1))
        rec.emit("vector", lambda e: e.tensor_mul(oc1[:], pO1[:], g["BaS"][:]),
                 waits=[("act", rec.count("act"))], inc=("dve", 1))
        rec.emit("vector", lambda e: e.tensor_mul(oc2[:], pO2[:], g["BbS"][:]),
                 waits=[], inc=("dve", 1))
        rec.emit("vector", lambda e: e.tensor_sub(oc1[:], oc1[:], oc2[:]),
                 waits=[], inc=("dve", 1))
        rec.emit("tensor", lambda e: e.matmul(
            zb[0:1, :], onesf32[:], oc1[:], start=True, stop=True),
            waits=[("dve", rec.count("dve"))], inc=("mm", 1))
        rec.emit("vector", lambda e: e.tensor_mul(oc2[:], oc1[:], oc1[:]),
                 waits=[], inc=("dve", 1))
        rec.emit("tensor", lambda e: e.matmul(
            zb[32:33, :], onesf32[:], oc2[:], start=True, stop=True),
            waits=[("dve", rec.count("dve"))], inc=("mm", 1))
        rec.emit("vector", lambda e: e.tensor_scalar(
            g["z1r"][:], zb[0:1, :], 1.0 / 128, None, ALU.mult),
            waits=[("mm", rec.count("mm"))], inc=("dve", 1))
        rec.emit("vector", lambda e: e.tensor_scalar(
            g["z2r"][:], zb[32:33, :], 1.0 / 128, None, ALU.mult),
            waits=[], inc=("dve", 1))
        rec.emit("vector", lambda e: e.tensor_mul(g["ar"][:], g["z1r"][:], g["z1r"][:]),
                 waits=[], inc=("dve", 1))
        rec.emit("vector", lambda e: e.tensor_sub(g["z2r"][:], g["z2r"][:], g["ar"][:]),
                 waits=[], inc=("dve", 1))
        rec.emit("scalar", lambda e: e.activation(
            g["br"][:], g["z2r"][:], AF.Sqrt, bias=g["epsrow"][:]),
            waits=[("dve", rec.count("dve"))], inc=("act", 1))
        rec.emit("vector", lambda e: e.reciprocal(g["ar"][:], g["br"][:]),
                 waits=[("act", rec.count("act"))], inc=("dve", 1))
        rec.emit("tensor", lambda e: e.matmul(
            pf[:], ones1x128[:], g["z1r"][:], start=True, stop=True),
            waits=[("dve", rec.count("dve"))], inc=("mm", 1))
        rec.emit("vector", lambda e: e.tensor_sub(oc1[:], oc1[:], pf[:]),
                 waits=[("mm", rec.count("mm"))], inc=("dve", 1))
        rec.emit("tensor", lambda e: e.matmul(
            pf[:], ones1x128[:], g["ar"][:], start=True, stop=True),
            waits=[("dve", rec.count("dve"))], inc=("mm", 1))
        rec.emit("vector", lambda e: e.tensor_mul(oc1[:], oc1[:], pf[:]),
                 waits=[("mm", rec.count("mm"))], inc=("dve", 1))
        rec.emit("vector", lambda e, h=h, b=b: e.tensor_scalar(
            OnT[h][:, b * 512:(b + 1) * 512], oc1[:],
            g["gnw_s"][:, h:h + 1], g["gnb_s"][:, h:h + 1], ALU.mult, ALU.add),
            waits=[], inc=("dve", 1))

    return emit_finalize


def emit_setup(rec, g):
    wf = g["w"]
    for nm, val in (("onesf32", 1.0), ("ones1x128", 1.0), ("onesb", 1.0),
                    ("epsrow", LN_EPS), ("epscol", LN_EPS),
                    ("qmax2", 0.0), ("kmax2", 0.0), ("pihalf", np.pi / 2)):
        rec.emit("vector", lambda e, nm=nm, val=val: e.memset(g[nm][:], val),
                 waits=[], inc=("dve", 1))
    for f in range(NF):
        rec.emit("sync", lambda e, f=f: e.dma_start(
            out=g["resid"][f][:], in_=g["x_d"][f * 128:(f + 1) * 128, :]),
            waits=[], inc=("sld", 16))
    rec.emit("sync", lambda e: e.dma_start(out=g["identb"][:], in_=wf["ident"]),
             waits=[], inc=("sld", 16))
    rec.emit("gpsimd", lambda e: e.dma_start(
        out=g["conv_s"][:], in_=wf["conv_w"].rearrange("(f p) t -> p f t", p=128)),
        waits=[], inc=("wld", 16))
    rec.emit("gpsimd", lambda e: e.dma_start(
        out=g["gnw_s"][:], in_=wf["gnw"]), waits=[], inc=("wld", 16))
    rec.emit("gpsimd", lambda e: e.dma_start(
        out=g["gnb_s"][:], in_=wf["gnb"]), waits=[], inc=("wld", 16))
    for i, nm in enumerate(("lq1", "lk1", "lq2", "lk2")):
        rec.emit("gpsimd", lambda e, i=i, nm=nm: e.dma_start(
            out=g["lrows"][:, i * 64:(i + 1) * 64], in_=wf[nm]),
            waits=[], inc=("wld", 16))
    rec.emit("vector", lambda e: e.tensor_scalar(
        g["gnw_s"][:], g["gnw_s"][:], 1.0 - LAMBDA_INIT, None, ALU.mult),
        waits=[("wld", rec.count("wld"))], inc=("dve", 1))
    rec.emit("vector", lambda e: e.tensor_scalar(
        g["gnb_s"][:], g["gnb_s"][:], 1.0 - LAMBDA_INIT, None, ALU.mult),
        waits=[], inc=("dve", 1))
    rec.emit("vector", lambda e: e.tensor_mul(
        g["lscr"][:, 0:64], g["lrows"][:, 0:64], g["lrows"][:, 64:128]),
        waits=[], inc=("dve", 1))
    rec.emit("vector", lambda e: e.tensor_mul(
        g["lscr"][:, 64:128], g["lrows"][:, 128:192], g["lrows"][:, 192:256]),
        waits=[], inc=("dve", 1))
    rec.emit("vector", lambda e: e.reduce_sum(
        g["lsum"][:, 0:1], g["lscr"][:, 0:64].rearrange("p (o w) -> p o w", o=1),
        axis=AXX), waits=[], inc=("dve", 1))
    rec.emit("vector", lambda e: e.reduce_sum(
        g["lsum"][:, 1:2], g["lscr"][:, 64:128].rearrange("p (o w) -> p o w", o=1),
        axis=AXX), waits=[], inc=("dve", 1))
    rec.emit("scalar", lambda e: e.activation(g["lsum"][:], g["lsum"][:], AF.Exp),
             waits=[("dve", rec.count("dve"))], inc=("act", 1))
    rec.emit("vector", lambda e: e.tensor_sub(
        g["lamv"][:], g["lsum"][:, 0:1], g["lsum"][:, 1:2]),
        waits=[("act", rec.count("act"))], inc=("dve", 1))
    rec.emit("vector", lambda e: e.tensor_scalar(
        g["lamv"][:], g["lamv"][:], 1.0, LAMBDA_INIT, ALU.mult, ALU.add),
        waits=[], inc=("dve", 1))
    rec.emit("gpsimd", lambda e: e.iota(
        g["iotar"][:], [[1, 32]], base=0, channel_multiplier=0,
        allow_small_or_imprecise_dtypes=True), waits=[], inc=("cc", 1))
    rec.emit("scalar", lambda e: e.activation(
        g["iotar"][:], g["iotar"][:], AF.Exp, scale=-float(np.log(10000.0) / 32.0)),
        waits=[("cc", rec.count("cc"))], inc=("act", 1))
    rec.emit("scalar", lambda e: e.activation(
        g["trig"][:, 0:32], g["iotar"][:], AF.Sin), waits=[], inc=("act", 1))
    rec.emit("scalar", lambda e: e.activation(
        g["trig"][:, 32:64], g["iotar"][:], AF.Sin), waits=[], inc=("act", 1))
    rec.emit("scalar", lambda e: e.activation(
        g["trig"][:, 64:96], g["iotar"][:], AF.Sin, bias=g["pihalf"][:]),
        waits=[("dve", rec.count("dve"))], inc=("act", 1))
    rec.emit("scalar", lambda e: e.activation(
        g["trig"][:, 96:128], g["iotar"][:], AF.Sin, bias=g["pihalf"][:]),
        waits=[], inc=("act", 1))
    rec.emit("tensor", lambda e: e.matmul(
        g["pbc"][:, 0:64], g["ones1x128"][:], g["trig"][:, 64:128],
        start=True, stop=True),
        waits=[("act", rec.count("act")), ("dve", rec.count("dve"))], inc=("mm", 1))
    rec.emit("scalar", lambda e: e.copy(g["cosb"][:], g["pbc"][:, 0:64]),
             waits=[("mm", rec.count("mm"))], inc=("act", 1))
    rec.emit("tensor", lambda e: e.matmul(
        g["pbc"][:, 0:64], g["ones1x128"][:], g["trig"][:, 0:64],
        start=True, stop=True),
        waits=[("act", rec.count("act"))], inc=("mm", 1))
    rec.emit("scalar", lambda e: e.copy(g["sinb"][:], g["pbc"][:, 0:64]),
             waits=[("mm", rec.count("mm"))], inc=("act", 1))


def emit_dumps(rec, g, nc, tag):
    for nm in _DUMPS:
        if nm.startswith("fsc:"):
            obj = g["fsc"].get(nm[4:])
        elif nm.startswith("dram:"):
            _, base, r0s, r1s = nm.split(":")
            src_ap = g[base].ap()[int(r0s):int(r1s), :]
            dd = nc.dram_tensor("dump_%s%s_%s" % (tag, base, r0s), list(src_ap.shape),
                                src_ap.dtype, kind="ExternalOutput").ap()
            rec.emit("sync", lambda e, src_ap=src_ap, dd=dd: e.dma_start(
                out=dd, in_=src_ap), inc=("sld", 16))
            continue
        else:
            obj = g.get(nm)
        if obj is None:
            continue
        nm = nm.replace(":", "_")
        tiles = obj if isinstance(obj, list) else [obj]
        aps = [t[:] if hasattr(t, "ap") or hasattr(t, "__enter__") or True else t
               for t in tiles]
        aps = []
        for t in tiles:
            try:
                aps.append(t[:])
            except TypeError:
                aps.append(t)
        rows = sum(a.shape[0] for a in aps)
        cols = max(int(np.prod(a.shape[1:])) for a in aps)
        dd = nc.dram_tensor("dump_" + tag + nm, [rows, cols], aps[0].dtype,
                            kind="ExternalOutput").ap()
        r0 = 0
        for a in aps:
            nr = a.shape[0]
            ncol = int(np.prod(a.shape[1:]))
            rec.emit("sync", lambda e, a=a, r0=r0, nr=nr, ncol=ncol, dd=dd: e.dma_start(
                out=dd[r0:r0 + nr, 0:ncol],
                in_=(a if a.ndim == 2 else a.rearrange(
                    "p a b -> p (a b)"))),
                inc=("sld", 16))
            r0 += nr


def emit_store(rec, g, dram, mark_done=True):
    for f in range(NF):
        rec.emit("sync", lambda e, f=f: e.dma_start(
            out=dram[f * 128:(f + 1) * 128, :], in_=g["resid"][f][:]),
            waits=(None if f == 0 else []), inc=("sld", 16))
    if mark_done:
        rec.emit("sync", lambda e: None, waits=[("sld", rec.count("sld"))])


def build_nc():
    nc = bass.Bass("TRN2", target_bir_lowering=False, debug=False,
                   detect_race_conditions=False)
    x_d = nc.dram_tensor("x", [D, TOK], F32, kind="ExternalInput").ap()
    y_d = nc.dram_tensor("y", [D, TOK], F32, kind="ExternalOutput").ap()
    w = {}
    for nm, shape, dt in (
        ("wq_a", [D, D], F32), ("wk_a", [D, D], F32), ("wv_a", [D, D], F32),
        ("wo_a", [D, D], F32),
        ("wq_f", [D, 1024], F32), ("wk_f", [D, 1024], F32), ("wv_f", [D, D], F32),
        ("wg_f", [D, D], F32), ("wout_f", [D, D], F32),
        ("conv_w", [D, 2], F32), ("router_w", [2, D], F32),
        ("lq1", [1, 64], F32), ("lk1", [1, 64], F32),
        ("lq2", [1, 64], F32), ("lk2", [1, 64], F32),
        ("gnw", [128, 16], F32), ("gnb", [128, 16], F32), ("ident", [128, 128], BF16),
    ):
        w[nm] = nc.dram_tensor(nm, shape, dt, kind="ExternalInput").ap()
    dbg = {ph: nc.dram_tensor("dbg_" + ph, [D, TOK], F32, kind="ExternalOutput").ap()
           for ph in _DEBUG}

    g = {"nc": nc, "w": w, "x_d": x_d}
    if "attn" in _PHASES:
        g.update({
            "k_sh": nc.dram_tensor("k_sh", [D, TOK], BF16),
            "v_sh": nc.dram_tensor("v_sh", [TOK, D], BF16),
            "k_all": nc.dram_tensor("k_all", [NCORES * D, TOK], BF16,
                                    addr_space="Shared"),
            "v_all": nc.dram_tensor("v_all", [NCORES * TOK, D], BF16,
                                    addr_space="Shared"),
            "mx_sh": nc.dram_tensor("mx_sh", [1, 2], F32),
            "mx_all": nc.dram_tensor("mx_all", [1, 2], F32, addr_space="Shared")})

    with contextlib.ExitStack() as stack:
        ec = stack.enter_context
        g["resid"] = [ec(nc.sbuf_tensor("res%d" % f, [128, TOK], F32)) for f in range(NF)]
        for nm, shape, dt in (
            ("onesf32", [128, 1], F32), ("ones1x128", [1, 128], F32),
            ("onesb", [128, 1], BF16), ("epsrow", [1, 1], F32),
            ("epscol", [128, 1], F32), ("identb", [128, 128], BF16),
            ("conv_s", [128, 16, 2], F32), ("gnw_s", [128, 16], F32),
            ("gnb_s", [128, 16], F32),
            ("cosb", [128, 64], F32), ("sinb", [128, 64], F32),
            ("lamv", [1, 1], F32),
            ("qmax2", [1, 1], F32), ("kmax2", [1, 1], F32), ("rmax", [1, 1], F32),
            ("mxrow", [1, 2], F32), ("negm", [1, 1], F32), ("negMcol", [128, 1], F32),
            ("meanB", [128, TOK], F32), ("rstdB", [128, TOK], F32),
        ):
            g[nm] = ec(nc.sbuf_tensor(nm, shape, dt))
        g["lnrows"] = [ec(nc.sbuf_tensor("lnr%d" % i, [1, 512], F32)) for i in range(4)]
        g["lnsq"] = [ec(nc.sbuf_tensor("lnsq%d" % i, [128, TOK], F32)) for i in range(2)]

        rec = Rec(nc, stack)
        with nc.Block() as block:
            with contextlib.ExitStack() as s0:
                for nm, shape in (("iotar", [1, 32]), ("trig", [1, 128]),
                                  ("pihalf", [1, 1]), ("lrows", [1, 256]),
                                  ("lscr", [1, 128]), ("lsum", [1, 2])):
                    g[nm] = s0.enter_context(nc.sbuf_tensor(nm, shape, F32))
                g["pbc"] = s0.enter_context(nc.psum_tensor("pbc0", [128, 512], F32))
                emit_setup(rec, g)
                rec.replay(block)

            for phase in _PHASES:
                if phase.startswith("ffn"):
                    with contextlib.ExitStack() as fs:
                        fe = fs.enter_context
                        pn = phase
                        g["yT"] = [fe(nc.sbuf_tensor("%s_yT%d" % (pn, f), [128, TOK], BF16))
                                   for f in range(NF)]
                        g["zT"] = [fe(nc.sbuf_tensor("%s_zT%d" % (pn, f), [128, TOK], BF16))
                                   for f in range(NF)]
                        g["wbuf"] = [fe(nc.sbuf_tensor("%s_wb%d" % (pn, i),
                                                       [128, NF, 256], BF16))
                                     for i in range(2)]
                        g["qb"] = [fe(nc.sbuf_tensor("%s_qb%d" % (pn, t), [128, 256], BF16))
                                   for t in range(NTT)]
                        g["kb"] = [fe(nc.sbuf_tensor("%s_kb%d" % (pn, t), [128, 256], BF16))
                                   for t in range(NTT)]
                        g["vb"] = [fe(nc.sbuf_tensor("%s_vb%d" % (pn, t), [128, 512], BF16))
                                   for t in range(NTT)]
                        g["ob"] = [g["lnsq"][0][:, 0:512], g["lnsq"][0][:, 512:1024]]
                        g["fscr"] = [g["lnsq"][1][:, 0:512], g["lnsq"][1][:, 512:1024]]
                        zstT = fe(nc.sbuf_tensor("%s_zst" % pn, [128, NTT, 512], BF16))
                        g["zst"] = [zstT[:, t, :] for t in range(NTT)]
                        _zf = zstT[:].rearrange("p a b -> p (a b)").bitcast(F32)
                        g["cscr"] = [_zf[:, 0:512], _zf[:, 512:1024]]
                        g["rwrow"] = fe(nc.sbuf_tensor("%s_rwr" % pn, [1, 2, 512], F32))
                        g["rwb"] = fe(nc.sbuf_tensor("%s_rwb" % pn, [128, 2, 512], F32))
                        full = "lg0 lg1 da s0 s1 oh0 oh1 ms0 ms1".split()
                        pairl = ("n0 n1 f0 f1 w00 w10 w11 t1s t2s "
                                 "d00 d10 d11 qk00 qk10 qk11").split()
                        g["fsc"] = {}
                        for nm in full:
                            g["fsc"][nm] = fe(nc.sbuf_tensor(
                                "%s_sc_%s" % (pn, nm), [128, NTT, 4], F32))
                        for nm in pairl:
                            g["fsc"][nm] = fe(nc.sbuf_tensor(
                                "%s_sc_%s" % (pn, nm), [128, 4, 4], F32))
                        for nm in ("muP", "vvP", "rsP", "t3P"):
                            g["fsc"][nm] = fe(nc.sbuf_tensor(
                                "%s_sc_%s" % (pn, nm), [128, 2, 4], F32))
                        g["pproj"] = [fs.enter_context(
                            nc.psum_tensor("%s_ppj%d" % (pn, i), [128, 512], F32))
                            for i in range(4)]
                        g["ptr"] = fs.enter_context(
                            nc.psum_tensor("%s_ptr" % pn, [128, 512], BF16))
                        g["prs"] = fs.enter_context(
                            nc.psum_tensor("%s_prs" % pn, [1, 512], F32))
                        g["prq"] = fs.enter_context(
                            nc.psum_tensor("%s_prq" % pn, [1, 512], F32))
                        g["pbc"] = fs.enter_context(
                            nc.psum_tensor("%s_pbc" % pn, [128, 512], F32))
                        emit_ffn(rec, g)
                        emit_dumps(rec, g, nc, pn)
                        rec.replay(block)
                elif phase == "attn":
                    with contextlib.ExitStack() as atop:
                        ae = atop.enter_context
                        g["qT"] = [ae(nc.sbuf_tensor("qT%d" % h, [128, TOK], BF16))
                                   for h in range(H)]
                        OnTT = ae(nc.sbuf_tensor("OnTT", [128, H, TOK], BF16))
                        g["OnT"] = [OnTT[:, h, :] for h in range(H)]
                        _qw = OnTT[:].rearrange("p a b -> p (a b)").bitcast(F32)
                        g["qwst"] = [
                            _qw[:, 0:2048].rearrange("p (f c) -> p f c", c=128),
                            _qw[:, 2048:4096].rearrange("p (f c) -> p f c", c=128)]
                        with contextlib.ExitStack() as aps:
                            pe = aps.enter_context
                            g["xT"] = [pe(nc.sbuf_tensor("xT%d" % f, [128, TOK], BF16))
                                       for f in range(NF)]
                            g["kst"] = [pe(nc.sbuf_tensor("kst%d" % i, [128, TOK], BF16))
                                        for i in range(2)]
                            g["vst"] = [pe(nc.sbuf_tensor("vst%d" % i, [128, 512], BF16))
                                        for i in range(2)]
                            g["sqb"] = [g["lnsq"][1][:].bitcast(BF16)[:, 0:TOK],
                                        g["lnsq"][0][:].bitcast(BF16)[:, 0:TOK]]
                            g["wbuf"] = [pe(nc.sbuf_tensor("awb%d" % i,
                                                           [128, NF, 256], BF16))
                                         for i in range(2)]

                            g["pproj"] = [aps.enter_context(
                                nc.psum_tensor("apj%d" % i, [128, 512], F32))
                                for i in range(4)]
                            g["prs"] = aps.enter_context(
                                nc.psum_tensor("aprs", [1, 512], F32))
                            g["prq"] = aps.enter_context(
                                nc.psum_tensor("aprq", [1, 512], F32))
                            g["pbc"] = aps.enter_context(
                                nc.psum_tensor("apbc", [128, 512], F32))
                            emit_attn_proj(rec, g)
                            rec.replay(block)
                        with contextlib.ExitStack() as acs:
                            ce = acs.enter_context
                            g["kbh"] = [ce(nc.sbuf_tensor("kbh%d" % i, [128, 4096], BF16))
                                        for i in range(2)]
                            g["vbh"] = [ce(nc.sbuf_tensor("vbh%d" % i, [128, 4096], BF16))
                                        for i in range(2)]
                            g["ptile"] = [ce(nc.sbuf_tensor("ptl%d" % i, [128, 1024], BF16))
                                          for i in range(2)]
                            g["oc1"] = ce(nc.sbuf_tensor("oc1", [128, 512], F32))
                            g["oc2"] = ce(nc.sbuf_tensor("oc2", [128, 512], F32))
                            g["BaS"] = ce(nc.sbuf_tensor("BaS", [128, 512], F32))
                            g["BbS"] = ce(nc.sbuf_tensor("BbS", [128, 512], F32))
                            g["z1r"] = ce(nc.sbuf_tensor("z1r", [1, 512], F32))
                            g["z2r"] = ce(nc.sbuf_tensor("z2r", [1, 512], F32))
                            g["ar"] = ce(nc.sbuf_tensor("ar", [1, 512], F32))
                            g["br"] = ce(nc.sbuf_tensor("br", [1, 512], F32))
                            g["sp"] = [acs.enter_context(
                                nc.psum_tensor("sp%d" % i, [128, 1024], F32))
                                for i in range(2)]
                            g["pO1"] = acs.enter_context(nc.psum_tensor("pO1", [128, 512], F32))
                            g["pO2"] = acs.enter_context(nc.psum_tensor("pO2", [128, 512], F32))
                            g["zb"] = acs.enter_context(nc.psum_tensor("zbk", [128, 512], F32))
                            g["pf"] = acs.enter_context(nc.psum_tensor("pfk", [128, 512], F32))
                            emit_attn_core(rec, g)
                            rec.replay(block)
                        with contextlib.ExitStack() as aws:
                            g["wbuf"] = [aws.enter_context(
                                nc.sbuf_tensor("owb%d" % i, [128, NF, 512], BF16))
                                for i in range(2)]
                            g["pproj"] = [aws.enter_context(
                                nc.psum_tensor("opj%d" % i, [128, 512], F32))
                                for i in range(4)]
                            emit_out_proj(rec, g, w["wo_a"], g["OnT"], 1.0)
                            emit_dumps(rec, g, nc, "at")
                            rec.replay(block)
                if phase in dbg:
                    emit_store(rec, g, dbg[phase], mark_done=False)
                    rec.replay(block)

            emit_store(rec, g, y_d)
            rec.replay(block)
    return nc


def kernel(**inputs):
    f32 = np.float32
    x = np.asarray(inputs["L0_attn_x"], dtype=f32)
    ident = np.eye(128, dtype=ml_dtypes.bfloat16)
    rw = np.asarray(inputs["L0_ffn_router_w"], dtype=f32)
    rw2 = np.ascontiguousarray(rw.transpose(2, 0, 1).reshape(2, D))
    base = {
        "wq_a": np.asarray(inputs["L0_attn_Wq"], dtype=f32),
        "wk_a": np.asarray(inputs["L0_attn_Wk"], dtype=f32),
        "wv_a": np.asarray(inputs["L0_attn_Wv"], dtype=f32),
        "wo_a": np.asarray(inputs["L0_attn_Wo"], dtype=f32),
        "wq_f": np.asarray(inputs["L0_ffn_Wq"], dtype=f32),
        "wk_f": np.asarray(inputs["L0_ffn_Wk"], dtype=f32),
        "wv_f": np.asarray(inputs["L0_ffn_Wv"], dtype=f32),
        "wg_f": np.asarray(inputs["L0_ffn_Wg"], dtype=f32),
        "wout_f": np.asarray(inputs["L0_ffn_Wout"], dtype=f32),
        "conv_w": np.asarray(inputs["L0_ffn_conv_w"], dtype=f32),
        "router_w": rw2,
        "lq1": np.asarray(inputs["L0_attn_lambda_q1"], dtype=f32).reshape(1, 64),
        "lk1": np.asarray(inputs["L0_attn_lambda_k1"], dtype=f32).reshape(1, 64),
        "lq2": np.asarray(inputs["L0_attn_lambda_q2"], dtype=f32).reshape(1, 64),
        "lk2": np.asarray(inputs["L0_attn_lambda_k2"], dtype=f32).reshape(1, 64),
        "gnw": np.ascontiguousarray(
            np.asarray(inputs["L0_attn_gn_weight"], dtype=f32).reshape(16, 128).T),
        "gnb": np.ascontiguousarray(
            np.asarray(inputs["L0_attn_gn_bias"], dtype=f32).reshape(16, 128).T),
        "ident": ident,
    }
    in_maps = []
    for c in range(NCORES):
        xc = np.concatenate([x[0, c * POS:(c + 1) * POS],
                             x[1, c * POS:(c + 1) * POS]], axis=0)
        m = dict(base)
        m["x"] = np.ascontiguousarray(xc.T)
        in_maps.append(m)

    nc = build_nc()
    res = run_bass_kernel_spmd(nc, in_maps, core_ids=list(range(NCORES)))
    out = np.empty((2, S, D), np.float32)
    for c, r in enumerate(res.results):
        y = np.asarray(r["y"]).T
        out[0, c * POS:(c + 1) * POS] = y[:POS]
        out[1, c * POS:(c + 1) * POS] = y[POS:]
    return out
